# version 04 (final)
# baseline (speedup 1.0000x reference)
import numpy as np
from contextlib import ExitStack

import concourse.bass as bass
import concourse.mybir as mybir
from concourse.bass_utils import run_bass_kernel_spmd

F32 = mybir.dt.float32
BF16 = mybir.dt.bfloat16
I32 = mybir.dt.int32
AF = mybir.ActivationFunctionType
ALU = mybir.AluOpType

NCORES = 8
D = 2048
TOK = 4096
HALO = 3
TA = TOK + HALO
CT = D // 128

H, P_HD, NST, KCONV, Q = 64, 64, 128, 4, 256
I_IN = 4096
CONV_DIM = I_IN + 2 * NST
OPROJ = I_IN + CONV_DIM + H
NCHUNK = TOK // Q
INTER = 8192

ENGINES = ("tensor", "vector", "scalar", "gpsimd", "sync")


class Sem:
    __slots__ = ("h", "n", "name")

    def __init__(self, h, name):
        self.h = h
        self.n = 0
        self.name = name


class Prog:

    def __init__(self, nc, es):
        self.nc = nc
        self.es = es
        self.stream = {e: [] for e in ENGINES}
        self.sems = {}

    def sem(self, name):
        if name not in self.sems:
            h = self.es.enter_context(self.nc.semaphore(name))
            self.sems[name] = Sem(h, name)
        return self.sems[name]

    def raw(self, eng, fn):
        self.stream[eng].append(fn)

    def wait(self, eng, sem, val=None):
        v = sem.n if val is None else val
        assert v <= sem.n, f"wait on future inc of {sem.name}: {v} > {sem.n}"
        if v <= 0:
            return
        self.stream[eng].append(lambda E, h=sem.h, v=v: E.wait_ge(h, v))

    def op(self, eng, fn, inc=None, inc_by=1):
        if inc is not None:
            inc.n += inc_by

            def g(E, fn=fn, h=inc.h, b=inc_by):
                fn(E).then_inc(h, b)

            self.stream[eng].append(g)
        else:
            self.stream[eng].append(lambda E, fn=fn: fn(E))

    def finalize(self):
        block = self.es.enter_context(self.nc.Block())
        for eng in ENGINES:
            ops = self.stream[eng]

            def body(E, ops=ops):
                for f in ops:
                    f(E)

            getattr(block, eng)(body)


class Ctx:

    def __init__(self, fused_mids=False):
        self.nc = bass.Bass("TRN2", target_bir_lowering=False, debug=False)
        self.es = ExitStack()
        self.es.__enter__()
        self.P = Prog(self.nc, self.es)
        self.dram = {}
        self.dram_kind = {}
        self.ready = {}
        self.fused_mids = fused_mids
        self.sb = {}
        self.bank_owner = {}
        self._init_done = False
        self._scope = None
        self._scope_alloc = []
        self.Ls = self.P.sem("Ls")
        self.Lg = self.P.sem("Lg")
        self.T = self.P.sem("T")
        self.V = self.P.sem("V")
        self.S = self.P.sem("S")
        self.G = self.P.sem("G")

    def _dram(self, name, shape, dtype, kind):
        if name not in self.dram:
            if kind is None:
                self.dram[name] = self.nc.dram_tensor(
                    name, list(shape), dtype).ap()
            else:
                self.dram[name] = self.nc.dram_tensor(
                    name, list(shape), dtype, kind=kind).ap()
            self.dram_kind[name] = kind or "mid"
        return self.dram[name]

    def ext(self, name, shape, dtype=F32):
        return self._dram(name, shape, dtype, "ExternalInput")

    def mid(self, name, shape, dtype=BF16):
        kind = None if self.fused_mids else "ExternalInput"
        return self._dram(name, shape, dtype, kind)

    def mid_out(self, name, shape, dtype=BF16):
        kind = None if self.fused_mids else "ExternalOutput"
        return self._dram(name, shape, dtype, kind)

    def out(self, name, shape, dtype=F32):
        return self._dram(name, shape, dtype, "ExternalOutput")

    def internal(self, name, shape, dtype=F32):
        return self._dram(name, shape, dtype, None)

    def set_ready(self, name, sem=None):
        sem = sem or self.Ls
        self.ready[name] = (sem, sem.n)

    def wait_ready(self, eng, name):
        if name in self.ready:
            sem, cnt = self.ready[name]
            self.P.wait(eng, sem, cnt)

    def sbuf(self, name, shape, dtype):
        if name not in self.sb:
            scope = self._scope if self._scope is not None else self.es
            self._uniq = getattr(self, "_uniq", 0) + 1
            self.sb[name] = scope.enter_context(
                self.nc.sbuf_tensor(f"{name}__{self._uniq}", list(shape),
                                    dtype))
            if self._scope is not None:
                self._scope_alloc.append(name)
        return self.sb[name]

    def phase(self, name):
        ctx = self

        class _Scope:
            def __enter__(self_s):
                ctx._scope = ExitStack()
                ctx._scope.__enter__()
                ctx._scope_alloc = []
                return ctx

            def __exit__(self_s, *a):
                ctx.join()
                for n in ctx._scope_alloc:
                    ctx.sb.pop(n, None)
                ctx._scope.__exit__(None, None, None)
                ctx._scope = None
                ctx._scope_alloc = []
                return False

        return _Scope()

    def join(self):
        sems = [self.Ls, self.Lg, self.T, self.V, self.S, self.G]
        for eng in ENGINES:
            for s in sems:
                self.P.wait(eng, s)

    def bank_wait(self, eng, bank):
        if bank in self.bank_owner:
            sem, cnt = self.bank_owner[bank]
            self.P.wait(eng, sem, cnt)

    def bank_set(self, bank, sem):
        self.bank_owner[bank] = (sem, sem.n)

    def init_common(self):
        if self._init_done:
            return
        self._init_done = True
        self.banks = [self.es.enter_context(
            self.nc.psum_tensor(f"ps{i}", [128, 512], F32)) for i in range(6)]
        self.psT = [self.es.enter_context(
            self.nc.psum_tensor(f"psT{i}", [128, 1024], BF16)) for i in range(2)]
        P = self.P
        saved_scope = self._scope
        self._scope = None
        self.ones_col = self.sbuf("ones_col", (128, 1), BF16)
        self.ones_row = self.sbuf("ones_row", (1, 128), BF16)
        self.ones_rowf = self.sbuf("ones_rowf", (1, 128), F32)
        self._scope = saved_scope
        P.op("vector", lambda E: E.memset(self.ones_col[:, :], 1.0))
        P.op("vector", lambda E: E.memset(self.ones_row[:, :], 1.0))
        P.op("vector", lambda E: E.memset(self.ones_rowf[:, :], 1.0), inc=self.V)

    def finish(self):
        self.P.wait("sync", self.Ls)
        self.P.wait("gpsimd", self.Lg)
        self.P.finalize()

    def close(self):
        self.es.__exit__(None, None, None)



def prepare_inputs(inputs):
    x = inputs["L0_attn_hidden_states"]
    vals = {}

    def shared(name, arr):
        vals[name] = [arr] * NCORES

    percore = {n: [] for n in ["x", "selS", "selH"]}
    for i in range(NCORES):
        b, h = i // 2, i % 2
        if h == 0:
            sl = np.concatenate(
                [np.zeros((HALO, D), np.float32), x[b, 0:TOK, :]], axis=0)
        else:
            sl = x[b, TOK - HALO:2 * TOK, :]
        percore["x"].append(np.ascontiguousarray(sl.T))
        selS = np.zeros((128, 8), np.float32)
        selH = np.zeros((128, 8), np.float32)
        if h == 1:
            selS[:, i - 1] = 1.0
            selH[:, i - 1] = 1.0
        percore["selS"].append(selS)
        percore["selH"].append(selH)
    vals.update(percore)

    tr = lambda a: np.ascontiguousarray(np.asarray(a).T)
    col = lambda a: np.ascontiguousarray(np.asarray(a).reshape(-1, 1))
    pcol = lambda a: np.ascontiguousarray(np.asarray(a).reshape(-1, 128).T)

    shared("w_in_t", tr(inputs["L0_attn_in_proj_w"]))
    shared("conv_w", np.ascontiguousarray(
        inputs["L0_attn_conv_w"].reshape(34, 128, 4)
        .transpose(1, 0, 2).reshape(128, 136)))
    shared("conv_b", pcol(inputs["L0_attn_conv_b"]))
    shared("dt_bias", col(inputs["L0_attn_dt_bias"]))
    shared("A_col", col(inputs["L0_attn_A"]))
    shared("Drep", pcol(np.repeat(inputs["L0_attn_D"], P_HD)))
    shared("norm_w", pcol(inputs["L0_attn_norm_w"]))
    shared("w_out_t", tr(inputs["L0_attn_out_proj_w"]))
    shared("su_ug", pcol(inputs["L0_ffn_SU_ug"]))
    shared("su_dp", pcol(inputs["L0_ffn_SU_dp"]))
    shared("wsc_up", pcol(inputs["L0_ffn_Wscale_ug"][:INTER]))
    shared("wsc_gate", pcol(inputs["L0_ffn_Wscale_ug"][INTER:]))
    shared("wsc_dp", pcol(inputs["L0_ffn_Wscale_dp"]))
    ii = np.arange(128)
    had = 1.0 - 2.0 * (
        np.bitwise_count(np.bitwise_and(ii[:, None], ii[None, :])) % 2)
    shared("h128", np.ascontiguousarray(had.astype(np.float32)))
    shared("w_up_t", tr(inputs["L0_ffn_W_up"]))
    shared("w_gate_t", tr(inputs["L0_ffn_W_gate"]))
    shared("w_down_t", tr(inputs["L0_ffn_W_down"]))
    shared("rms_w", pcol(inputs["L1_attn_rms_w"]))
    shared("wf_t", tr(inputs["L1_attn_Wf_w"]))
    shared("wf_b", pcol(inputs["L1_attn_Wf_b"]))
    shared("wzg_t", tr(inputs["L1_attn_Wzg_w"]))
    shared("wz_t", tr(inputs["L1_attn_Wz_w"]))
    shared("wga_t", tr(inputs["L1_attn_Wga_w"]))
    shared("wgb_t", tr(inputs["L1_attn_Wgb_w"]))
    shared("wout_t", tr(inputs["L1_attn_Wout_w"]))
    shared("wout_b", pcol(inputs["L1_attn_Wout_b"]))
    shared("wg1_t", tr(inputs["L1_ffn_gate_w"]))
    shared("sg1", pcol(inputs["L1_ffn_gate_scale"]))
    shared("wu1_t", tr(inputs["L1_ffn_up_w"]))
    shared("su1", pcol(inputs["L1_ffn_up_scale"]))
    shared("wd1_t", tr(inputs["L1_ffn_down_w"]))
    shared("sd1", pcol(inputs["L1_ffn_down_scale"]))
    return vals



class WeightStream:

    def __init__(self, K, name, dram_ap, C, O, ow=128, nbuf=3):
        self.K = K
        self.dram = dram_ap
        self.C, self.O, self.ow = C, O, ow
        self.KT = C // 128
        self.nslab = (O + ow - 1) // ow
        self.nbuf = nbuf
        self.bufs = [K.sbuf(f"{name}_b{i}", (128, self.KT * ow), BF16)
                     for i in range(nbuf)]
        self.buf_free = [None] * nbuf
        self.loaded = [None] * self.nslab
        self.queue = "gpsimd"

    def slab_width(self, j):
        return min(self.ow, self.O - j * self.ow)

    def issue(self, j):
        K, P = self.K, self.K.P
        bi = j % self.nbuf
        assert j < self.nbuf or self.buf_free[bi] is not None, \
            f"prefetch too deep for {self.nbuf} buffers (slab {j})"
        if self.buf_free[bi] is not None:
            sem, cnt = self.buf_free[bi]
            P.wait(self.queue, sem, cnt)
        w = self.slab_width(j)
        buf = self.bufs[bi]
        src = self.dram[:, j * self.ow: j * self.ow + w]
        src = src.rearrange("(k p) o -> p k o", p=128)
        dst = buf[:, 0:self.KT * w].rearrange("p (k o) -> p k o", k=self.KT)
        P.op(self.queue, lambda E, d=dst, s=src: E.dma_start(out=d, in_=s),
             inc=K.Lg, inc_by=16)
        self.loaded[j] = (K.Lg, K.Lg.n)

    def use(self, j, eng="tensor"):
        K, P = self.K, self.K.P
        assert self.loaded[j] is not None
        sem, cnt = self.loaded[j]
        P.wait(eng, sem, cnt)
        buf = self.bufs[j % self.nbuf]
        w = self.slab_width(j)
        return lambda k, buf=buf, w=w: buf[:, k * w:(k + 1) * w]

    def release(self, j, sem):
        self.buf_free[j % self.nbuf] = (sem, sem.n)



def ph_ln(K, src_name, dst_name, src_dtype=F32, tcols=TA, src_is_ext=True,
          limit_chunks=None):
    P = K.P
    src = K.ext(src_name, (D, tcols), src_dtype) if src_is_ext else \
        K.mid(src_name, (D, tcols), src_dtype)
    dst = K.mid_out(dst_name, (D, tcols), BF16)
    K.init_common()
    ones_col, ones_row = K.ones_col, K.ones_row

    xt = K.sbuf("ln_xt", (128, CT * 512), F32)
    xb = K.sbuf("ln_xb", (128, CT * 512), BF16)
    xsq = K.sbuf("ln_xsq", (128, CT * 512), BF16)
    xn = K.sbuf("ln_xn", (128, CT * 512), BF16)
    srow = K.sbuf("ln_srow", (1, 512 * 5), F32)
    p_sum, p_sq, p_rb, p_mb = K.banks[0], K.banks[1], K.banks[2], K.banks[3]

    if tcols == TA:
        chunks = [(i * 512, 512) for i in range(TOK // 512)] + [(TA - 512, 512)]
    else:
        chunks = [(i * 512, 512) for i in range(tcols // 512)]
    if limit_chunks is not None:
        chunks = chunks[:limit_chunks]

    for (t0, w) in chunks:
        P.wait("sync", K.V)
        K.wait_ready("sync", src_name)
        P.op("sync", lambda E, t0=t0, w=w: E.dma_start(
            out=xt[:, 0:CT * w].rearrange("p (u t) -> p u t", u=CT),
            in_=src[:, t0:t0 + w].rearrange("(u p) t -> p u t", p=128)),
            inc=K.Ls, inc_by=16)
        P.wait("vector", K.Ls)
        P.wait("vector", K.T)
        for u in range(CT):
            P.op("vector", lambda E, u=u, w=w: E.tensor_copy(
                xb[:, u * w:(u + 1) * w], xt[:, u * w:(u + 1) * w]))
        for u in range(CT):
            P.op("vector", lambda E, u=u, w=w: E.tensor_tensor(
                xsq[:, u * w:(u + 1) * w], xt[:, u * w:(u + 1) * w],
                xt[:, u * w:(u + 1) * w], ALU.mult),
                inc=K.V if u == CT - 1 else None)
        K.bank_wait("tensor", "b0")
        K.bank_wait("tensor", "b1")
        P.wait("tensor", K.V)
        for u in range(CT):
            P.op("tensor", lambda E, u=u, w=w: E.matmul(
                p_sum[0:1, 0:w], ones_col[:, :], xb[:, u * w:(u + 1) * w],
                start=(u == 0), stop=(u == CT - 1)),
                inc=K.T if u == CT - 1 else None)
        for u in range(CT):
            P.op("tensor", lambda E, u=u, w=w: E.matmul(
                p_sq[0:1, 0:w], ones_col[:, :], xsq[:, u * w:(u + 1) * w],
                start=(u == 0), stop=(u == CT - 1)),
                inc=K.T if u == CT - 1 else None)
        mean = srow[0:1, 0:512]
        ex2 = srow[0:1, 512:1024]
        var = srow[0:1, 1024:1536]
        rstd = srow[0:1, 1536:2048]
        mrs = srow[0:1, 2048:2560]
        P.wait("vector", K.T)
        P.op("vector", lambda E, w=w: E.tensor_scalar(
            mean[:, 0:w], p_sum[0:1, 0:w], 1.0 / D, None, ALU.mult))
        P.op("vector", lambda E, w=w: E.tensor_scalar(
            ex2[:, 0:w], p_sq[0:1, 0:w], 1.0 / D, None, ALU.mult))
        P.op("vector", lambda E, w=w: E.tensor_tensor(
            var[:, 0:w], mean[:, 0:w], mean[:, 0:w], ALU.mult))
        P.op("vector", lambda E, w=w: E.tensor_tensor(
            var[:, 0:w], ex2[:, 0:w], var[:, 0:w], ALU.subtract))
        P.op("vector", lambda E, w=w: E.tensor_scalar(
            var[:, 0:w], var[:, 0:w], 1e-5, None, ALU.add), inc=K.V)
        K.bank_set("b0", K.V)
        K.bank_set("b1", K.V)
        P.wait("scalar", K.V)
        P.op("scalar", lambda E, w=w: E.activation(
            var[:, 0:w], var[:, 0:w], AF.Ln))
        P.op("scalar", lambda E, w=w: E.activation(
            rstd[:, 0:w], var[:, 0:w], AF.Exp, scale=-0.5), inc=K.S)
        P.wait("vector", K.S)
        P.op("vector", lambda E, w=w: E.tensor_tensor(
            mrs[:, 0:w], mean[:, 0:w], rstd[:, 0:w], ALU.mult), inc=K.V)
        K.bank_wait("tensor", "b2")
        K.bank_wait("tensor", "b3")
        P.wait("tensor", K.V)
        P.op("tensor", lambda E, w=w: E.matmul(
            p_rb[:, 0:w], K.ones_rowf[:, :], rstd[:, 0:w], start=True, stop=True))
        P.op("tensor", lambda E, w=w: E.matmul(
            p_mb[:, 0:w], K.ones_rowf[:, :], mrs[:, 0:w], start=True, stop=True),
            inc=K.T)
        P.wait("vector", K.T)
        P.wait("vector", K.Ls)
        for u in range(CT):
            P.op("vector", lambda E, u=u, w=w: E.tensor_tensor(
                xn[:, u * w:(u + 1) * w], xt[:, u * w:(u + 1) * w],
                p_rb[:, 0:w], ALU.mult))
        for u in range(CT):
            P.op("vector", lambda E, u=u, w=w: E.tensor_tensor(
                xn[:, u * w:(u + 1) * w], xn[:, u * w:(u + 1) * w],
                p_mb[:, 0:w], ALU.subtract),
                inc=K.V if u == CT - 1 else None)
        K.bank_set("b2", K.V)
        K.bank_set("b3", K.V)
        P.wait("sync", K.V)
        P.op("sync", lambda E, t0=t0, w=w: E.dma_start(
            out=dst[:, t0:t0 + w].rearrange("(u p) t -> p u t", p=128),
            in_=xn[:, 0:CT * w].rearrange("p (u t) -> p u t", u=CT)),
            inc=K.Ls, inc_by=16)
    K.set_ready(dst_name)



def ph_inproj(K):
    P = K.P
    xn = K.mid("XN", (D, TA), BF16)
    w_in = K.ext("w_in_t", (D, OPROJ), F32)
    dt_bias = K.ext("dt_bias", (H, 1), F32)
    gates = K.mid_out("GATES", (I_IN, TOK), BF16)
    xbc = K.mid_out("XBC", (CONV_DIM, TA), BF16)
    dt_out = K.mid_out("DT", (H, TOK), F32)
    K.init_common()

    ws = WeightStream(K, "wip", w_in, D, OPROJ, ow=128)
    mv = [K.sbuf("ip_mv0", (128, CT * 2048), BF16),
          K.sbuf("ip_mv1", (128, CT * 2048), BF16)]
    halo = K.sbuf("ip_halo", (128, CT * 128), BF16)
    ev = [K.sbuf("ip_ev0", (128, 2048 + 8), BF16),
          K.sbuf("ip_ev1", (128, 2048 + 8), BF16)]
    dtb_sb = K.sbuf("ip_dtb", (H, 1), F32)
    dtw = K.sbuf("ip_dtw", (H, 4 * 2048 + 4 * 512), F32)
    dtf = K.sbuf("ip_dtf", (H, 2048), F32)
    banks = K.banks
    ev_store = [0, 0]

    K.wait_ready("sync", "XN")
    P.op("sync", lambda E: E.dma_start(
        out=halo[:, :].rearrange("p (u t) -> p u t", u=CT),
        in_=xn[:, 0:128].rearrange("(u p) t -> p u t", p=128)),
        inc=K.Ls, inc_by=16)
    P.op("sync", lambda E: E.dma_start(out=dtb_sb[:, :], in_=dt_bias),
         inc=K.Ls, inc_by=16)
    halo_ld = K.Ls.n

    NSLAB = ws.nslab
    for th in range(2):
        t0 = HALO + th * 2048
        K.wait_ready("sync", "XN")
        P.op("sync", lambda E, t0=t0, th=th: E.dma_start(
            out=mv[th][:, :].rearrange("p (u t) -> p u t", u=CT),
            in_=xn[:, t0:t0 + 2048].rearrange("(u p) t -> p u t", p=128)),
            inc=K.Ls, inc_by=16)
        mv_ld = K.Ls.n
        ws.issue(0)
        ws.issue(1)
        for j in range(NSLAB):
            if j + 2 < NSLAB:
                ws.issue(j + 2)
            tile = ws.use(j)
            P.wait("tensor", K.Ls, mv_ld)
            ow = ws.slab_width(j)
            is_dt = (j == NSLAB - 1)
            e = ev[j % 2]
            for tq in range(4):
                bk = f"b{tq}"
                K.bank_wait("tensor", bk)
                for k in range(CT):
                    P.op("tensor", lambda E, k=k, tq=tq, ow=ow,
                         tile=tile, th=th: E.matmul(
                             banks[tq][0:ow, 0:512], tile(k),
                             mv[th][:, k * 2048 + tq * 512:
                                    k * 2048 + (tq + 1) * 512],
                             start=(k == 0), stop=(k == CT - 1)),
                         inc=K.T if k == CT - 1 else None)
                mm_done = K.T.n
                if j < 32:
                    P.wait("scalar", K.T, mm_done)
                    P.wait("scalar", K.Ls, ev_store[j % 2])
                    P.op("scalar", lambda E, tq=tq, ow=ow, e=e: E.activation(
                        e[0:ow, HALO + tq * 512: HALO + (tq + 1) * 512],
                        banks[tq][0:ow, 0:512], AF.Silu), inc=K.S)
                    K.bank_set(bk, K.S)
                elif not is_dt:
                    P.wait("vector", K.T, mm_done)
                    P.wait("vector", K.Ls, ev_store[j % 2])
                    P.op("vector", lambda E, tq=tq, ow=ow, e=e: E.tensor_copy(
                        e[0:ow, HALO + tq * 512: HALO + (tq + 1) * 512],
                        banks[tq][0:ow, 0:512]), inc=K.V)
                    K.bank_set(bk, K.V)
                else:
                    vv = dtw[:, tq * 512:(tq + 1) * 512]
                    mm = dtw[:, 2048 + tq * 512: 2048 + (tq + 1) * 512]
                    aa = dtw[:, 4096 + tq * 512: 4096 + (tq + 1) * 512]
                    ll = dtw[:, 6144 + tq * 512: 6144 + (tq + 1) * 512]
                    P.wait("vector", K.T, mm_done)
                    P.wait("vector", K.Ls)
                    P.op("vector", lambda E, vv=vv, tq=tq: E.tensor_scalar(
                        vv, banks[tq][0:H, 0:512], dtb_sb[:, 0:1],
                        None, ALU.add))
                    P.op("vector", lambda E, vv=vv, mm=mm: E.tensor_scalar(
                        mm, vv, 0.0, None, ALU.max), inc=K.V)
                    K.bank_set(bk, K.V)
                    P.wait("scalar", K.V)
                    P.op("scalar", lambda E, vv=vv, aa=aa: E.activation(
                        aa, vv, AF.Abs))
                    P.op("scalar", lambda E, aa=aa, ll=ll: E.activation(
                        ll, aa, AF.Exp, scale=-1.0), inc=K.S)
                    P.wait("vector", K.S)
                    P.op("vector", lambda E, ll=ll: E.tensor_scalar(
                        ll, ll, 1.0, None, ALU.add), inc=K.V)
                    P.wait("scalar", K.V)
                    P.op("scalar", lambda E, ll=ll: E.activation(
                        ll, ll, AF.Ln), inc=K.S)
                    P.wait("vector", K.S)
                    P.op("vector", lambda E, mm=mm, ll=ll, tq=tq:
                         E.tensor_tensor(dtf[:, tq * 512:(tq + 1) * 512],
                                         mm, ll, ALU.add), inc=K.V)
            if 32 <= j < 66:
                K.bank_wait("tensor", "b4")
                P.wait("tensor", K.Ls, halo_ld)
                for k in range(CT):
                    P.op("tensor", lambda E, k=k, tile=tile: E.matmul(
                        banks[4][:, 0:HALO], tile(k),
                        halo[:, k * 128: k * 128 + HALO],
                        start=(k == 0), stop=(k == CT - 1)),
                        inc=K.T if k == CT - 1 else None)
                P.wait("vector", K.T)
                P.op("vector", lambda E, e=e: E.tensor_copy(
                    e[:, 0:HALO], banks[4][:, 0:HALO]), inc=K.V)
                K.bank_set("b4", K.V)
            ws.release(j, K.T)
            if j < 32:
                P.wait("sync", K.S)
                P.op("sync", lambda E, j=j, th=th, e=e: E.dma_start(
                    out=gates[j * 128:(j + 1) * 128,
                              th * 2048:(th + 1) * 2048],
                    in_=e[:, HALO:HALO + 2048]), inc=K.Ls, inc_by=16)
            elif not is_dt:
                P.wait("sync", K.V)
                r0 = (j - 32) * 128
                if th == 0:
                    P.op("sync", lambda E, r0=r0, e=e: E.dma_start(
                        out=xbc[r0:r0 + 128, 0:HALO + 2048],
                        in_=e[:, 0:HALO + 2048]), inc=K.Ls, inc_by=16)
                else:
                    P.op("sync", lambda E, r0=r0, e=e: E.dma_start(
                        out=xbc[r0:r0 + 128, HALO + 2048:HALO + 4096],
                        in_=e[:, HALO:HALO + 2048]), inc=K.Ls, inc_by=16)
            else:
                P.wait("sync", K.V)
                P.op("sync", lambda E, th=th: E.dma_start(
                    out=dt_out[:, th * 2048:(th + 1) * 2048],
                    in_=dtf[:, 0:2048]), inc=K.Ls, inc_by=16)
            ev_store[j % 2] = K.Ls.n
    K.set_ready("GATES")
    K.set_ready("XBC")
    K.set_ready("DT")



def ph_conv(K):
    P = K.P
    xbc = K.mid("XBC", (CONV_DIM, TA), BF16)
    conv_w = K.ext("conv_w", (128, 34 * KCONV), F32)
    conv_b = K.ext("conv_b", (128, 34), F32)
    xbcc = K.mid_out("XBCC", (CONV_DIM, TOK), BF16)
    K.init_common()
    banks = K.banks
    NU = CONV_DIM // 128

    xin = [K.sbuf(f"cv_x{i}", (128, TA + 8), BF16) for i in range(2)]
    acc = [K.sbuf(f"cv_a{i}", (128, TOK), F32) for i in range(4)]
    outt = [K.sbuf(f"cv_o{i}", (128, TOK), BF16) for i in range(2)]
    cw = K.sbuf("cv_w", (128, NU * KCONV), F32)
    cb = K.sbuf("cv_b", (128, NU), F32)
    P.op("sync", lambda E: E.dma_start(out=cw[:, :], in_=conv_w),
         inc=K.Ls, inc_by=16)
    P.op("sync", lambda E: E.dma_start(out=cb[:, :], in_=conv_b),
         inc=K.Ls, inc_by=16)
    cwb_ld = K.Ls.n
    st_done = [0, 0]
    for u in range(NU):
        bi = u % 2
        P.wait("sync", K.S, None)
        K.wait_ready("sync", "XBC")
        P.wait("sync", K.V)
        P.op("sync", lambda E, u=u, bi=bi: E.dma_start(
            out=xin[bi][:, 0:TA], in_=xbc[u * 128:(u + 1) * 128, :]),
            inc=K.Ls, inc_by=16)
        ld = K.Ls.n
        P.wait("vector", K.Ls, ld)
        P.wait("vector", K.Ls, cwb_ld)
        P.wait("vector", K.S)
        a0, a1 = acc[2 * bi], acc[2 * bi + 1]
        P.op("vector", lambda E, u=u, bi=bi, a0=a0: E.tensor_scalar(
            a0[:, :], xin[bi][:, 0:TOK], cw[:, u * KCONV: u * KCONV + 1],
            cb[:, u:u + 1], ALU.mult, ALU.add))
        P.op("vector", lambda E, u=u, bi=bi, a0=a0, a1=a1:
             E.scalar_tensor_tensor(a1[:, :], xin[bi][:, 1:TOK + 1],
                                    cw[:, u * KCONV + 1: u * KCONV + 2],
                                    a0[:, :], ALU.mult, ALU.add))
        P.op("vector", lambda E, u=u, bi=bi, a0=a0, a1=a1:
             E.scalar_tensor_tensor(a0[:, :], xin[bi][:, 2:TOK + 2],
                                    cw[:, u * KCONV + 2: u * KCONV + 3],
                                    a1[:, :], ALU.mult, ALU.add))
        P.op("vector", lambda E, u=u, bi=bi, a0=a0, a1=a1:
             E.scalar_tensor_tensor(a1[:, :], xin[bi][:, 3:TOK + 3],
                                    cw[:, u * KCONV + 3: u * KCONV + 4],
                                    a0[:, :], ALU.mult, ALU.add), inc=K.V)
        P.wait("scalar", K.V)
        P.wait("scalar", K.Ls, st_done[bi])
        P.op("scalar", lambda E, bi=bi, a1=a1: E.activation(
            outt[bi][:, :], a1[:, :], AF.Silu), inc=K.S)
        P.wait("sync", K.S)
        P.op("sync", lambda E, u=u, bi=bi: E.dma_start(
            out=xbcc[u * 128:(u + 1) * 128, :], in_=outt[bi][:, :]),
            inc=K.Ls, inc_by=16)
        st_done[bi] = K.Ls.n
    K.set_ready("XBCC")


def ph_decay(K):
    P = K.P
    dt_in = K.mid("DT", (H, TOK), F32)
    a_col = K.ext("A_col", (H, 1), F32)
    dac_d = K.mid_out("DAC", (H, TOK), F32)
    eaf_d = K.mid_out("EAF", (H, TOK), BF16)
    eacf_d = K.mid_out("EACF", (H, TOK), BF16)
    drh_d = K.mid_out("DRH", (H, TOK), BF16)
    drl_d = K.mid_out("DRL", (H, TOK), BF16)
    dlt_d = K.mid_out("DLT", (TOK, H), F32)
    wdet_d = K.mid_out("WDET", (TOK, H), F32)
    cdet_d = K.mid_out("CDET", (NCHUNK, H), F32)
    K.init_common()
    banks = K.banks

    dt_sb = K.sbuf("dc_dt", (H, TOK), F32)
    da = K.sbuf("dc_da", (H, TOK), F32)
    dac = K.sbuf("dc_dac", (H, TOK), F32)
    zer = K.sbuf("dc_zero", (H, Q), F32)
    wde = K.sbuf("dc_wde", (H, TOK), F32)
    drel = K.sbuf("dc_drel", (H, TOK), F32)
    drh = K.sbuf("dc_drh", (H, TOK), BF16)
    drl = K.sbuf("dc_drl", (H, TOK), BF16)
    eaf = K.sbuf("dc_eaf", (H, TOK), BF16)
    eacf = K.sbuf("dc_eacf", (H, TOK), BF16)
    lndt = K.sbuf("dc_lndt", (H, TOK), F32)
    dace = K.sbuf("dc_dace", (H, NCHUNK), F32)
    lc = K.sbuf("dc_lc", (H, NCHUNK + 1), F32)
    aa_sb = K.sbuf("dc_a", (H, 1), F32)
    tmp = K.sbuf("dc_tmp", (H, TOK), F32)
    trev = K.sbuf("dc_trev", (128, 3 * H), F32)
    identf = K.sbuf("identf", (128, 128), F32)
    ioti = K.sbuf("ioti", (128, 128), I32)

    K.wait_ready("sync", "DT")
    P.op("sync", lambda E: E.dma_start(out=dt_sb[:, :], in_=dt_in),
         inc=K.Ls, inc_by=16)
    P.op("sync", lambda E: E.dma_start(out=aa_sb[:, :], in_=a_col),
         inc=K.Ls, inc_by=16)
    P.op("gpsimd", lambda E: E.iota(ioti[:, :], [[1, 128]],
                                    channel_multiplier=-1,
                                    allow_small_or_imprecise_dtypes=True),
         inc=K.G)
    P.wait("vector", K.G)
    P.op("vector", lambda E: E.tensor_scalar(identf[:, :], ioti[:, :], 0,
                                             None, ALU.is_equal))
    P.op("vector", lambda E: E.memset(zer[:, :], 0.0), inc=K.V)
    P.wait("vector", K.Ls)
    P.op("vector", lambda E: E.tensor_scalar(
        da[:, :], dt_sb[:, :], aa_sb[:, 0:1], None, ALU.mult))
    for c in range(NCHUNK):
        P.op("vector", lambda E, c=c: E.tensor_tensor_scan(
            dac[:, c * Q:(c + 1) * Q], da[:, c * Q:(c + 1) * Q], zer[:, :],
            zer[:, 0:1], ALU.add, ALU.add))
    for c in range(NCHUNK):
        P.op("vector", lambda E, c=c: E.tensor_copy(
            dace[:, c:c + 1], dac[:, c * Q + Q - 1: c * Q + Q]))
    P.op("vector", lambda E: E.memset(lc[:, 0:1], 0.0))
    P.op("vector", lambda E: E.tensor_tensor_scan(
        lc[:, 1:NCHUNK + 1], dace[:, :], zer[:, 0:NCHUNK], zer[:, 0:1],
        ALU.add, ALU.add))
    for c in range(NCHUNK):
        P.op("vector", lambda E, c=c: E.tensor_scalar(
            drel[:, c * Q:(c + 1) * Q], dac[:, c * Q:(c + 1) * Q],
            dace[:, c:c + 1], None, ALU.subtract))
    for c in range(NCHUNK):
        P.op("vector", lambda E, c=c: E.tensor_scalar(
            tmp[:, c * Q:(c + 1) * Q], dac[:, c * Q:(c + 1) * Q],
            lc[:, c:c + 1], None, ALU.add), inc=K.V if c == NCHUNK - 1 else None)
    P.wait("scalar", K.V)
    P.op("scalar", lambda E: E.activation(eaf[:, :], dac[:, :], AF.Exp))
    P.op("scalar", lambda E: E.activation(eacf[:, :], tmp[:, :], AF.Exp))
    P.op("scalar", lambda E: E.activation(lndt[:, :], dt_sb[:, :], AF.Ln),
         inc=K.S)
    P.wait("vector", K.S)
    P.op("vector", lambda E: E.tensor_scalar(
        tmp[:, :], drel[:, :], -1.0, None, ALU.mult), inc=K.V)
    P.wait("scalar", K.V)
    P.op("scalar", lambda E: E.activation(tmp[:, :], tmp[:, :], AF.Exp),
         inc=K.S)
    P.wait("vector", K.S)
    P.op("vector", lambda E: E.tensor_tensor(
        wde[:, :], tmp[:, :], dt_sb[:, :], ALU.mult))
    P.op("vector", lambda E: E.tensor_tensor(
        lndt[:, :], drel[:, :], lndt[:, :], ALU.subtract))
    P.op("vector", lambda E: E.tensor_copy(drh[:, :], drel[:, :]))
    P.op("vector", lambda E: E.tensor_tensor(
        drl[:, :], drel[:, :], drh[:, :], ALU.subtract), inc=K.V)
    P.wait("scalar", K.V)
    P.op("scalar", lambda E: E.activation(dace[:, :], dace[:, :], AF.Exp),
         inc=K.S)
    P.wait("sync", K.V)
    P.wait("sync", K.S)
    for nm, t in [("DAC", dac), ("EAF", eaf), ("EACF", eacf),
                  ("DRH", drh), ("DRL", drl)]:
        P.op("sync", lambda E, nm=nm, t=t: E.dma_start(
            out=K.dram[nm], in_=t[:, :]), inc=K.Ls, inc_by=16)
    tr_ev = trev
    trev_st = [0, 0, 0]
    for gi, (nm, t) in enumerate([("DLT", lndt), ("WDET", wde)]):
        for j in range(TOK // 128):
            bk = f"b{j % 2}"
            sl = (gi * (TOK // 128) + j) % 3
            K.bank_wait("tensor", bk)
            P.wait("tensor", K.V)
            P.wait("tensor", K.S)
            P.op("tensor", lambda E, t=t, j=j: E.transpose(
                banks[j % 2][0:128, 0:H], t[:, j * 128:(j + 1) * 128],
                identf[0:H, 0:H]), inc=K.T)
            P.wait("vector", K.T)
            P.wait("vector", K.Ls, trev_st[sl])
            P.op("vector", lambda E, j=j, sl=sl: E.tensor_copy(
                tr_ev[:, sl * H:sl * H + H],
                banks[j % 2][0:128, 0:H]), inc=K.V)
            K.bank_set(bk, K.V)
            P.wait("sync", K.V)
            P.op("sync", lambda E, nm=nm, j=j, sl=sl: E.dma_start(
                out=K.dram[nm][j * 128:(j + 1) * 128, :],
                in_=tr_ev[:, sl * H:sl * H + H]),
                inc=K.Ls, inc_by=16)
            trev_st[sl] = K.Ls.n
    K.bank_wait("tensor", "b2")
    P.wait("tensor", K.S)
    P.op("tensor", lambda E: E.transpose(
        banks[2][0:NCHUNK, 0:H], dace[:, 0:NCHUNK],
        identf[0:H, 0:H]), inc=K.T)
    P.wait("vector", K.T)
    P.op("vector", lambda E: E.tensor_copy(
        tr_ev[0:NCHUNK, 0:H], banks[2][0:NCHUNK, 0:H]), inc=K.V)
    K.bank_set("b2", K.V)
    P.wait("sync", K.V)
    P.op("sync", lambda E: E.dma_start(
        out=K.dram["CDET"], in_=tr_ev[0:NCHUNK, 0:H]), inc=K.Ls, inc_by=16)
    for nm in ["DAC", "EAF", "EACF", "DRH", "DRL", "DLT", "WDET", "CDET"]:
        K.set_ready(nm)



def ph_ssd(K, nchunks=NCHUNK, do_cc=True, dbg=False):
    P = K.P
    xbcc = K.mid("XBCC", (CONV_DIM, TOK), BF16)
    drh_d = K.mid("DRH", (H, TOK), BF16)
    drl_d = K.mid("DRL", (H, TOK), BF16)
    eaf_d = K.mid("EAF", (H, TOK), BF16)
    dlt_d = K.mid("DLT", (TOK, H), F32)
    wdet_d = K.mid("WDET", (TOK, H), F32)
    cdet_d = K.mid("CDET", (NCHUNK, H), F32)
    drep = K.ext("Drep", (128, 32), F32)
    y_d = K.mid_out("Y", (I_IN, TOK), BF16)
    ccs_in = K.internal("CCSIN", (128, I_IN), BF16)
    ccs_out = K.internal("CCSG", (NCORES * 128, I_IN), BF16)
    K.init_common()
    banks = K.banks
    psT = K.psT

    s_run = K.sbuf("sd_srun", (128, I_IN), F32)
    s_runb = K.sbuf("sd_srunb", (128, I_IN), BF16)
    xch_t = K.sbuf("sd_xch", (128, 32 * Q), BF16)
    xch = [xch_t, xch_t]
    bct_t = K.sbuf("sd_bct", (128, 2 * Q), BF16)
    bct = [bct_t, bct_t]
    xt = K.sbuf("sd_xt", (128, 2 * I_IN), BF16)
    xpt = K.sbuf("sd_xpt", (128, 2 * I_IN), BF16)
    bts = K.sbuf("sd_bts", (128, 2 * 128), BF16)
    drhf_t = K.sbuf("sd_drhf", (33, 32 * Q), BF16)
    drlf_t = K.sbuf("sd_drlf", (33, 32 * Q), BF16)
    eaff_t = K.sbuf("sd_eaff", (33, 32 * Q), BF16)
    drhf = [drhf_t, drhf_t]
    drlf = [drlf_t, drlf_t]
    eaff = [eaff_t, eaff_t]
    ctl_t = K.sbuf("sd_ctl", (128, 4 * H), F32)
    ctl = [ctl_t, ctl_t]
    cdrow_t = K.sbuf("sd_cdrow", (1, H), F32)
    cdrow = [cdrow_t, cdrow_t]
    cdcol = K.sbuf("sd_cdcol", (128, H), F32)
    m01 = K.sbuf("sd_m01", (128, 2 * Q), BF16)
    onesb = K.sbuf("sd_onesb", (128, 2 * Q), BF16)
    v1t = K.sbuf("sd_v1", (128, 2 * 512), F32)
    e2t = K.sbuf("sd_e2", (128, 2 * 512), BF16)
    mtt = K.sbuf("sd_mt", (128, 2 * 512), BF16)
    ceat = K.sbuf("sd_cea", (128, 2 * Q), BF16)
    cbm = K.sbuf("sd_cbm", (128, 2 * Q), BF16)
    ych_t = K.sbuf("sd_ych", (128, 32 * Q), BF16)
    ych = [ych_t, ych_t]
    ident = K.sbuf("sd_ident", (128, 128), BF16)
    ones33 = K.sbuf("sd_ones33", (33, 128), BF16)
    ioti2 = K.sbuf("sd_ioti", (128, 128), I32)

    P.op("gpsimd", lambda E: E.iota(ioti2[:, :], [[1, 128]],
                                    channel_multiplier=-1,
                                    allow_small_or_imprecise_dtypes=True),
         inc=K.G)
    P.wait("vector", K.G)
    P.op("vector", lambda E: E.tensor_scalar(ident[:, :], ioti2[:, :], 0,
                                             None, ALU.is_equal))
    P.op("vector", lambda E: E.memset(onesb[:, :], 1.0))
    P.op("vector", lambda E: E.memset(ones33[:, :], 1.0))
    P.op("vector", lambda E: E.memset(s_run[:, :], 0.0))
    P.op("vector", lambda E: E.memset(s_runb[:, :], 0.0), inc=K.V)
    P.wait("gpsimd", K.V)
    P.op("gpsimd", lambda E: E.affine_select(
        m01[:, 0:Q], onesb[:, 0:Q], [[1, Q]], ALU.is_ge, 0.0,
        channel_multiplier=-1), inc=K.G)
    P.op("gpsimd", lambda E: E.affine_select(
        m01[:, Q:2 * Q], onesb[:, 0:Q], [[1, Q]], ALU.is_ge, 0.0,
        base=-128, channel_multiplier=-1), inc=K.G)
    drep_sb = K.sbuf("sd_drep", (128, 32), F32)
    P.op("sync", lambda E: E.dma_start(out=drep_sb[:, :], in_=drep),
         inc=K.Ls, inc_by=16)

    def colap(buf, grp, sh, h):
        return buf[:, (2 * grp + sh) * H + h: (2 * grp + sh) * H + h + 1]

    for c in range(nchunks):
        cc = 0
        cq = slice(c * Q, (c + 1) * Q)
        for key in (f"xch{cc}", f"bct{cc}", f"fl{cc}", f"ctl{cc}"):
            K.bank_wait("sync", key)
        K.wait_ready("sync", "XBCC")
        P.op("sync", lambda E, c=c, cc=cc: E.dma_start(
            out=xch[cc][:, :].rearrange("p (u t) -> p u t", u=32),
            in_=xbcc[0:I_IN, c * Q:(c + 1) * Q].rearrange(
                "(u p) t -> p u t", p=128)), inc=K.Ls, inc_by=16)
        P.op("sync", lambda E, c=c, cc=cc: E.dma_start(
            out=bct[cc][:, :].rearrange("p (u t) -> p u t", u=2),
            in_=xbcc[I_IN:I_IN + 2 * NST, c * Q:(c + 1) * Q].rearrange(
                "(u p) t -> p u t", p=128)), inc=K.Ls, inc_by=16)
        xch_ld = K.Ls.n
        for nm, buf in (("DRH", drhf[cc]), ("DRL", drlf[cc]), ("EAF", eaff[cc])):
            K.wait_ready("sync", nm)
            for half in range(2):
                P.op("sync", lambda E, nm=nm, buf=buf, c=c, half=half:
                     E.dma_start(
                         out=buf[32 * half:32 * half + 1, :].rearrange(
                             "o (h t) -> o h t", h=32),
                         in_=K.dram[nm][32 * half:32 * (half + 1),
                                        c * Q:(c + 1) * Q].rearrange(
                             "(o h) t -> o h t", o=1)),
                     inc=K.Ls, inc_by=16)
        for gi, nm in enumerate(("DLT", "WDET")):
            K.wait_ready("sync", nm)
            P.op("sync", lambda E, nm=nm, gi=gi, cc=cc, c=c: E.dma_start(
                out=ctl[cc][:, 2 * gi * H:2 * (gi + 1) * H].rearrange(
                    "p (s h) -> p s h", s=2),
                in_=K.dram[nm][c * Q:(c + 1) * Q, :].rearrange(
                    "(s p) h -> p s h", p=128)), inc=K.Ls, inc_by=16)
        K.wait_ready("sync", "CDET")
        P.op("sync", lambda E, c=c, cc=cc: E.dma_start(
            out=cdrow[cc][:, :], in_=cdet_d[c:c + 1, :]), inc=K.Ls, inc_by=16)
        ld_all = K.Ls.n

        K.bank_wait("tensor", "b0")
        P.wait("tensor", K.Ls, ld_all)
        P.op("tensor", lambda E, cc=cc: E.matmul(
            banks[0][0:128, 0:H], K.ones_rowf[:, :], cdrow[cc][:, :],
            start=True, stop=True), inc=K.T)
        P.wait("vector", K.T)
        K.bank_wait("vector", "cdcol")
        P.op("vector", lambda E: E.tensor_copy(cdcol[:, :],
                                               banks[0][0:128, 0:H]), inc=K.V)
        K.bank_set("b0", K.V)

        P.wait("tensor", K.Ls, xch_ld)
        for sh in range(2):
            key = f"psTb{sh}"
            K.bank_wait("tensor", key)
            P.op("tensor", lambda E, sh=sh: E.transpose(
                psT[sh][:, 0:128],
                bct[cc][:, sh * 128:(sh + 1) * 128], ident[:, :]), inc=K.T)
            P.wait("vector", K.T)
            P.op("vector", lambda E, sh=sh: E.tensor_copy(
                bts[:, sh * 128:(sh + 1) * 128],
                psT[sh][:, 0:128]), inc=K.V)
            K.bank_set(key, K.V)
        K.bank_wait("tensor", "xt")
        for u in range(32):
            for sh in range(2):
                key = f"psTb{sh}"
                K.bank_wait("tensor", key)
                P.op("tensor", lambda E, u=u, sh=sh, cc=cc: E.transpose(
                    psT[sh][:, 512:640],
                    xch[cc][:, u * Q + sh * 128: u * Q + (sh + 1) * 128],
                    ident[:, :]), inc=K.T)
                P.wait("vector", K.T)
                P.op("vector", lambda E, u=u, sh=sh: E.tensor_copy(
                    xt[:, sh * I_IN + u * 128: sh * I_IN + (u + 1) * 128],
                    psT[sh][:, 512:640]),
                    inc=K.V)
                K.bank_set(key, K.V)
        xt_done = K.V.n

        K.bank_wait("vector", "xpt")
        P.wait("vector", K.Ls, ld_all)
        for sh in range(2):
            for h in range(H):
                P.op("vector", lambda E, sh=sh, h=h, cc=cc: E.tensor_scalar(
                    xpt[:, sh * I_IN + h * P_HD: sh * I_IN + (h + 1) * P_HD],
                    xt[:, sh * I_IN + h * P_HD: sh * I_IN + (h + 1) * P_HD],
                    colap(ctl[cc], 1, sh, h), None, ALU.mult),
                    inc=K.V if (sh == 1 and h == H - 1) else None)
        xpt_done = K.V.n

        K.bank_wait("tensor", "b0")
        P.wait("tensor", K.Ls, xch_ld)
        for sh in range(2):
            P.op("tensor", lambda E, sh=sh, cc=cc: E.matmul(
                banks[0][:, sh * Q:(sh + 1) * Q],
                bct[cc][:, sh * 128:(sh + 1) * 128],
                bct[cc][:, Q:2 * Q], start=True, stop=True),
                inc=K.T)
        cb_done = K.T.n
        P.wait("vector", K.T, cb_done)
        P.wait("vector", K.G)
        K.bank_wait("vector", "cbm")
        for sh in range(2):
            P.op("vector", lambda E, sh=sh: E.tensor_tensor(
                cbm[:, sh * Q:(sh + 1) * Q], banks[0][:, sh * Q:(sh + 1) * Q],
                m01[:, sh * Q:(sh + 1) * Q], ALU.mult), inc=K.V)
        K.bank_set("b0", K.V)
        cbm_done = K.V.n

        for h in range(H):
            par = h % 2
            hp = h // 2
            po = 64 * par
            ybk = 3 + (hp % 2)
            dbk = 1 + par
            K.bank_wait("tensor", f"Db{par}")
            P.wait("tensor", K.Ls, ld_all)
            hb = 32 * (h // 32)
            P.op("tensor", lambda E, h=h, dbk=dbk, cc=cc, hb=hb: E.matmul(
                banks[dbk][:, 0:Q], ones33[hb:hb + 1, :],
                drhf[cc][hb:hb + 1, (h % 32) * Q:(h % 32 + 1) * Q],
                start=True, stop=False))
            P.op("tensor", lambda E, h=h, dbk=dbk, cc=cc, hb=hb: E.matmul(
                banks[dbk][:, 0:Q], ones33[hb:hb + 1, :],
                drlf[cc][hb:hb + 1, (h % 32) * Q:(h % 32 + 1) * Q],
                start=False, stop=True),
                inc=K.T)
            dbc_done = K.T.n
            K.bank_wait("vector", f"v1r{par}")
            P.wait("vector", K.T, dbc_done)
            P.wait("vector", K.V, cbm_done)
            for sh in range(2):
                P.op("vector", lambda E, h=h, dbk=dbk, par=par, sh=sh, cc=cc:
                     E.tensor_scalar(
                         v1t[:, par * 512 + sh * Q: par * 512 + (sh + 1) * Q],
                         banks[dbk][:, 0:Q],
                         colap(ctl[cc], 0, sh, h),
                         88.0,
                         ALU.subtract, ALU.min),
                     inc=K.V if sh == 1 else None)
            K.bank_set(f"Db{par}", K.V)
            v1_done = K.V.n
            K.bank_wait("scalar", f"e2r{par}")
            P.wait("scalar", K.V, v1_done)
            for sh in range(2):
                P.op("scalar", lambda E, par=par, sh=sh: E.activation(
                    e2t[:, par * 512 + sh * Q: par * 512 + (sh + 1) * Q],
                    v1t[:, par * 512 + sh * Q: par * 512 + (sh + 1) * Q],
                    AF.Exp), inc=K.S if sh == 1 else None)
            K.bank_set(f"v1r{par}", K.S)
            e2_done = K.S.n
            K.bank_wait("vector", f"mtr{par}")
            P.wait("vector", K.S, e2_done)
            for sh in range(2):
                P.op("vector", lambda E, par=par, sh=sh: E.tensor_tensor(
                    mtt[:, par * 512 + sh * Q: par * 512 + (sh + 1) * Q],
                    e2t[:, par * 512 + sh * Q: par * 512 + (sh + 1) * Q],
                    cbm[:, sh * Q:(sh + 1) * Q], ALU.mult),
                    inc=K.V if sh == 1 else None)
            K.bank_set(f"e2r{par}", K.V)
            mt_done = K.V.n
            if dbg and c == 0 and h == 0:
                dbgt = K.sbuf("sd_dbg", (128, 512), F32)
                dbg_d = K.mid_out("DBG", (128, 6 * 512), F32)
                P.wait("vector", K.T, dbc_done)
                P.op("vector", lambda E: E.tensor_copy(dbgt[:, 0:Q],
                                                       banks[1][:, 0:Q]),
                     inc=K.V)
                P.wait("sync", K.V)
                P.op("sync", lambda E: E.dma_start(
                    out=K.dram["DBG"][:, 0:512], in_=dbgt[:, :]),
                    inc=K.Ls, inc_by=16)
                P.wait("sync", K.V, mt_done)
                P.op("sync", lambda E: E.dma_start(
                    out=K.dram["DBG"][:, 512:1024],
                    in_=v1t[:, 0:512]), inc=K.Ls, inc_by=16)
                P.wait("gpsimd", K.S, e2_done)
                P.op("gpsimd", lambda E: E.dma_start(
                    out=K.dram["DBG"][:, 1024:1536],
                    in_=e2t[:, 0:512]), inc=K.Lg, inc_by=16)
                P.wait("gpsimd", K.V, mt_done)
                P.op("gpsimd", lambda E: E.dma_start(
                    out=K.dram["DBG"][:, 1536:2048],
                    in_=mtt[:, 0:512]), inc=K.Lg, inc_by=16)
                P.wait("gpsimd", K.V, cbm_done)
                P.op("gpsimd", lambda E: E.dma_start(
                    out=K.dram["DBG"][:, 2048:2560],
                    in_=cbm[:, 0:512]), inc=K.Lg, inc_by=16)
                P.op("gpsimd", lambda E: E.dma_start(
                    out=K.dram["DBG"][:, 2560:3072],
                    in_=m01[:, 0:512]), inc=K.Lg, inc_by=16)
            ebk = 5 if par == 0 else 0
            K.bank_wait("tensor", f"b{ebk}")
            P.op("tensor", lambda E, h=h, ebk=ebk, cc=cc, hb=hb: E.matmul(
                banks[ebk][:, 0:Q], ones33[hb:hb + 1, :],
                eaff[cc][hb:hb + 1, (h % 32) * Q:(h % 32 + 1) * Q],
                start=True, stop=True),
                inc=K.T)
            eab_done = K.T.n
            K.bank_wait("vector", f"cear{par}")
            P.wait("vector", K.T, eab_done)
            P.op("vector", lambda E, par=par, ebk=ebk, cc=cc: E.tensor_tensor(
                ceat[:, par * Q:(par + 1) * Q], bct[cc][:, Q:2 * Q],
                banks[ebk][:, 0:Q], ALU.mult), inc=K.V)
            K.bank_set(f"b{ebk}", K.V)
            cea_done = K.V.n
            if par == 0:
                K.bank_wait("tensor", f"b{ybk}")
            P.wait("tensor", K.V, mt_done)
            P.wait("tensor", K.V, cea_done)
            P.wait("tensor", K.V, xt_done)
            P.op("tensor", lambda E, h=h, par=par, po=po, ybk=ybk: E.matmul(
                banks[ybk][po:po + P_HD, 0:Q],
                xt[:, 0 * I_IN + h * P_HD: 0 * I_IN + (h + 1) * P_HD],
                mtt[:, par * 512 + 0 * Q: par * 512 + 1 * Q],
                start=True, stop=False))
            P.op("tensor", lambda E, h=h, par=par, po=po, ybk=ybk: E.matmul(
                banks[ybk][po:po + P_HD, 0:Q],
                xt[:, 1 * I_IN + h * P_HD: 1 * I_IN + (h + 1) * P_HD],
                mtt[:, par * 512 + 1 * Q: par * 512 + 2 * Q],
                start=False, stop=False))
            P.op("tensor", lambda E, h=h, par=par, po=po, ybk=ybk: E.matmul(
                banks[ybk][po:po + P_HD, 0:Q],
                s_runb[:, h * P_HD:(h + 1) * P_HD],
                ceat[:, par * Q:(par + 1) * Q],
                start=False, stop=True), inc=K.T)
            K.bank_set(f"mtr{par}", K.T)
            K.bank_set(f"cear{par}", K.T)
            y_done = K.T.n
            if par == 1:
                P.wait("vector", K.T, y_done)
                K.bank_wait("vector", f"ych{cc}")
                P.op("vector", lambda E, hp=hp, cc=cc, ybk=ybk:
                     E.scalar_tensor_tensor(
                         ych[cc][:, hp * Q:(hp + 1) * Q],
                         xch[cc][:, hp * Q:(hp + 1) * Q],
                         drep_sb[:, hp:hp + 1],
                         banks[ybk][:, 0:Q], ALU.mult, ALU.add), inc=K.V)
                K.bank_set(f"b{ybk}", K.V)
        yev_done = K.V.n

        for hg in range(8):
            sbk = 3 + (hg % 2)
            K.bank_wait("tensor", f"b{sbk}")
            P.wait("tensor", K.V, xpt_done)
            for hh in range(8):
                h = hg * 8 + hh
                for sh in range(2):
                    P.op("tensor", lambda E, h=h, hh=hh, sh=sh, sbk=sbk:
                         E.matmul(
                             banks[sbk][:, hh * P_HD:(hh + 1) * P_HD],
                             bts[:, sh * 128:(sh + 1) * 128],
                             xpt[:, sh * I_IN + h * P_HD:
                                 sh * I_IN + (h + 1) * P_HD],
                             start=(sh == 0), stop=(sh == 1)),
                         inc=K.T if (hh == 7 and sh == 1) else None)
            st_done = K.T.n
            P.wait("vector", K.T, st_done)
            for hh in range(8):
                h = hg * 8 + hh
                P.op("vector", lambda E, h=h, hh=hh, sbk=sbk:
                     E.scalar_tensor_tensor(
                         s_run[:, h * P_HD:(h + 1) * P_HD],
                         s_run[:, h * P_HD:(h + 1) * P_HD],
                         cdcol[:, h:h + 1],
                         banks[sbk][:, hh * P_HD:(hh + 1) * P_HD],
                         ALU.mult, ALU.add),
                     inc=K.V if hh == 7 else None)
            K.bank_set(f"b{sbk}", K.V)
        K.bank_set("cdcol", K.V)
        K.bank_set("xpt", K.T)
        K.bank_set("xt", K.T)
        P.op("vector", lambda E: E.tensor_copy(s_runb[:, :], s_run[:, :]),
             inc=K.V)

        P.wait("sync", K.V, yev_done)
        P.op("sync", lambda E, c=c, cc=cc: E.dma_start(
            out=y_d[:, c * Q:(c + 1) * Q].rearrange("(u p) t -> p u t", p=128),
            in_=ych[cc][:, :].rearrange("p (u t) -> p u t", u=32)),
            inc=K.Ls, inc_by=16)
        K.bank_set(f"ych{cc}", K.Ls)
        K.bank_set(f"xch{cc}", K.V)
        K.bank_set(f"bct{cc}", K.V)
        K.bank_set(f"fl{cc}", K.T)
        K.bank_set(f"ctl{cc}", K.V)

    if not do_cc:
        K.set_ready("Y")
        return
    P.wait("sync", K.V)
    P.op("sync", lambda E: E.dma_start(out=ccs_in, in_=s_runb[:, :]),
         inc=K.Ls, inc_by=16)
    P.wait("gpsimd", K.Ls)
    CC1 = K.P.sem("CC1")
    P.raw("gpsimd", lambda E: E.collective_compute(
        "AllGather", ALU.bypass, replica_groups=[list(range(NCORES))],
        ins=[ccs_in.opt()], outs=[ccs_out.opt()]).then_inc(CC1.h))
    CC1.n += 1
    K.cc1 = CC1
    K.set_ready("Y")



def ph_ssd2(K):
    P = K.P
    xbcc = K.mid("XBCC", (CONV_DIM, TOK), BF16)
    eacf_d = K.mid("EACF", (H, TOK), BF16)
    y_d = K.mid("Y", (I_IN, TOK), BF16)
    sels = K.ext("selS", (128, 8), F32)
    ccs_out = K.internal("CCSG", (NCORES * 128, I_IN), BF16)
    K.init_common()
    banks = K.banks

    sbnd = K.sbuf("s2_sbnd", (128, I_IN), BF16)
    acc = [K.sbuf(f"s2_acc{i}", (128, I_IN), F32) for i in range(2)]
    gt = K.sbuf("s2_gt", (128, I_IN), BF16)
    sel_sb = K.sbuf("s2_sel", (128, 8), F32)
    ctile_t = K.sbuf("s2_c", (128, Q), BF16)
    ctile = [ctile_t, ctile_t]
    eacff_t = K.sbuf("s2_eacf", (33, 32 * Q), BF16)
    eacff = [eacff_t, eacff_t]
    ych_t2 = K.sbuf("s2_ych", (128, 32 * Q), BF16)
    ych = [ych_t2, ych_t2]
    ceat = K.sbuf("s2_cea", (128, 2 * Q), BF16)
    ones33b = K.sbuf("s2_ones33", (33, 128), BF16)
    P.op("vector", lambda E: E.memset(ones33b[:, :], 1.0), inc=K.V)

    P.op("sync", lambda E: E.dma_start(out=sel_sb[:, :], in_=sels),
         inc=K.Ls, inc_by=16)
    if hasattr(K, "cc1"):
        P.wait("gpsimd", K.cc1, 1)
    for j in range(NCORES):
        P.wait("gpsimd", K.V)
        P.op("gpsimd", lambda E, j=j: E.dma_start(
            out=gt[:, :], in_=ccs_out[j * 128:(j + 1) * 128, :]),
            inc=K.Lg, inc_by=16)
        P.wait("vector", K.Lg)
        P.wait("vector", K.Ls)
        if j == 0:
            P.op("vector", lambda E: E.tensor_scalar(
                acc[0][:, :], gt[:, :], sel_sb[:, 0:1], None, ALU.mult),
                inc=K.V)
        else:
            P.op("vector", lambda E, j=j: E.scalar_tensor_tensor(
                acc[j % 2][:, :], gt[:, :], sel_sb[:, j:j + 1],
                acc[(j - 1) % 2][:, :], ALU.mult, ALU.add), inc=K.V)
    P.op("vector", lambda E: E.tensor_copy(sbnd[:, :], acc[7 % 2][:, :]),
         inc=K.V)
    sbnd_done = K.V.n

    for c in range(NCHUNK):
        cc = 0
        K.bank_wait("sync", f"ld{cc}")
        K.wait_ready("sync", "Y")
        P.op("sync", lambda E, c=c, cc=cc: E.dma_start(
            out=ctile[cc][:, :],
            in_=xbcc[I_IN + NST:I_IN + 2 * NST, c * Q:(c + 1) * Q]),
            inc=K.Ls, inc_by=16)
        for half in range(2):
            P.op("sync", lambda E, c=c, cc=cc, half=half: E.dma_start(
                out=eacff[cc][32 * half:32 * half + 1, :].rearrange(
                    "o (h t) -> o h t", h=32),
                in_=eacf_d[32 * half:32 * (half + 1),
                           c * Q:(c + 1) * Q].rearrange(
                    "(o h) t -> o h t", o=1)), inc=K.Ls, inc_by=16)
        P.op("sync", lambda E, c=c, cc=cc: E.dma_start(
            out=ych[cc][:, :].rearrange("p (u t) -> p u t", u=32),
            in_=y_d[:, c * Q:(c + 1) * Q].rearrange(
                "(u p) t -> p u t", p=128)), inc=K.Ls, inc_by=16)
        ld_all = K.Ls.n
        for h in range(H):
            par = h % 2
            hp = h // 2
            po = 64 * par
            ybk = 3 + (hp % 2)
            ebk = 2 if par == 0 else 5
            K.bank_wait("tensor", f"b{ebk}")
            P.wait("tensor", K.Ls, ld_all)
            hb = 32 * (h // 32)
            P.op("tensor", lambda E, h=h, ebk=ebk, cc=cc, hb=hb: E.matmul(
                banks[ebk][:, 0:Q], ones33b[hb:hb + 1, :],
                eacff[cc][hb:hb + 1, (h % 32) * Q:(h % 32 + 1) * Q],
                start=True, stop=True),
                inc=K.T)
            eab_done = K.T.n
            K.bank_wait("vector", f"cear{par}")
            P.wait("vector", K.T, eab_done)
            P.wait("vector", K.Ls, ld_all)
            P.op("vector", lambda E, par=par, ebk=ebk, cc=cc: E.tensor_tensor(
                ceat[:, par * Q:(par + 1) * Q], ctile[cc][:, :],
                banks[ebk][:, 0:Q], ALU.mult), inc=K.V)
            K.bank_set(f"b{ebk}", K.V)
            cea_done = K.V.n
            if par == 0:
                K.bank_wait("tensor", f"b{ybk}")
            P.wait("tensor", K.V, cea_done)
            P.wait("tensor", K.V, sbnd_done)
            P.op("tensor", lambda E, h=h, po=po, par=par, ybk=ybk: E.matmul(
                banks[ybk][po:po + P_HD, 0:Q],
                sbnd[:, h * P_HD:(h + 1) * P_HD],
                ceat[:, par * Q:(par + 1) * Q], start=True, stop=True),
                inc=K.T)
            K.bank_set(f"cear{par}", K.T)
            ycorr_done = K.T.n
            if par == 1:
                P.wait("vector", K.T, ycorr_done)
                P.op("vector", lambda E, hp=hp, cc=cc, ybk=ybk: E.tensor_tensor(
                    ych[cc][:, hp * Q:(hp + 1) * Q],
                    ych[cc][:, hp * Q:(hp + 1) * Q],
                    banks[ybk][:, 0:Q], ALU.add), inc=K.V)
                K.bank_set(f"b{ybk}", K.V)
        P.wait("sync", K.V)
        P.op("sync", lambda E, c=c, cc=cc: E.dma_start(
            out=y_d[:, c * Q:(c + 1) * Q].rearrange("(u p) t -> p u t", p=128),
            in_=ych[cc][:, :].rearrange("p (u t) -> p u t", u=32)),
            inc=K.Ls, inc_by=16)
        K.bank_set(f"ld{cc}", K.Ls)
    K.ready["Y"] = (K.Ls, K.Ls.n)



def ph_gateout(K):
    P = K.P
    y_d = K.mid("Y", (I_IN, TOK), BF16)
    gates = K.mid("GATES", (I_IN, TOK), BF16)
    norm_w = K.ext("norm_w", (128, 32), F32)
    w_out = K.ext("w_out_t", (I_IN, D), F32)
    attn0 = K.mid_out("ATTN0", (D, TOK), BF16)
    K.init_common()
    banks = K.banks
    TQ = 1024
    NU = 32

    g = K.sbuf("go_g", (128, NU * TQ), BF16)
    g2 = K.sbuf("go_g2", (128, TQ), BF16)
    st = [K.sbuf(f"go_st{i}", (128, 2 * TQ), BF16) for i in range(2)]
    nw = K.sbuf("go_nw", (128, 32), F32)
    srow = K.sbuf("go_srow", (1, 2 * TQ), F32)
    ev = [K.sbuf(f"go_ev{i}", (128, TQ), BF16) for i in range(2)]
    ws = WeightStream(K, "go_w", w_out, I_IN, D, ow=128)
    P.op("sync", lambda E: E.dma_start(out=nw[:, :], in_=norm_w),
         inc=K.Ls, inc_by=16)
    ev_st = [0, 0]

    for tq in range(TOK // TQ):
        t0 = tq * TQ
        K.bank_wait("tensor", "b0")
        K.bank_wait("tensor", "b5")
        for u in range(NU):
            bi = u % 2
            P.wait("sync", K.V)
            K.wait_ready("sync", "Y")
            K.wait_ready("sync", "GATES")
            P.op("sync", lambda E, u=u, t0=t0, bi=bi: E.dma_start(
                out=st[bi][:, 0:TQ],
                in_=y_d[u * 128:(u + 1) * 128, t0:t0 + TQ]),
                inc=K.Ls, inc_by=16)
            P.op("sync", lambda E, u=u, t0=t0, bi=bi: E.dma_start(
                out=st[bi][:, TQ:2 * TQ],
                in_=gates[u * 128:(u + 1) * 128, t0:t0 + TQ]),
                inc=K.Ls, inc_by=16)
            P.wait("vector", K.Ls)
            P.wait("vector", K.T)
            P.op("vector", lambda E, u=u, bi=bi: E.tensor_tensor(
                g[:, u * TQ:(u + 1) * TQ], st[bi][:, 0:TQ],
                st[bi][:, TQ:2 * TQ], ALU.mult))
            P.op("vector", lambda E, u=u: E.tensor_tensor(
                g2[:, :], g[:, u * TQ:(u + 1) * TQ],
                g[:, u * TQ:(u + 1) * TQ], ALU.mult), inc=K.V)
            P.wait("tensor", K.V)
            for half in range(2):
                hbk = 0 if half == 0 else 5
                P.op("tensor", lambda E, u=u, half=half, hbk=hbk: E.matmul(
                    banks[hbk][0:1, 0:512], K.ones_col[:, :],
                    g2[:, half * 512:(half + 1) * 512],
                    start=(u == 0), stop=(u == NU - 1)),
                    inc=K.T if (u == NU - 1 and half == 1) else None)
        mean = srow[0:1, 0:TQ]
        P.wait("vector", K.T)
        P.op("vector", lambda E: E.tensor_scalar(
            mean[:, 0:512], banks[0][0:1, 0:512], 1.0 / I_IN, 1e-5,
            ALU.mult, ALU.add))
        P.op("vector", lambda E: E.tensor_scalar(
            mean[:, 512:1024], banks[5][0:1, 0:512], 1.0 / I_IN, 1e-5,
            ALU.mult, ALU.add), inc=K.V)
        K.bank_set("b0", K.V)
        K.bank_set("b5", K.V)
        P.wait("scalar", K.V)
        P.op("scalar", lambda E: E.activation(mean[:, :], mean[:, :], AF.Ln))
        P.op("scalar", lambda E: E.activation(mean[:, :], mean[:, :],
                                              AF.Exp, scale=-0.5), inc=K.S)
        K.bank_wait("tensor", "b1")
        K.bank_wait("tensor", "b4")
        P.wait("tensor", K.S)
        P.op("tensor", lambda E: E.matmul(
            banks[1][:, 0:512], K.ones_rowf[:, :], mean[:, 0:512],
            start=True, stop=True))
        P.op("tensor", lambda E: E.matmul(
            banks[4][:, 0:512], K.ones_rowf[:, :], mean[:, 512:1024],
            start=True, stop=True), inc=K.T)
        rb_done = K.T.n
        P.wait("vector", K.T, rb_done)
        for u in range(NU):
            for half in range(2):
                rbk = 1 if half == 0 else 4
                P.op("vector", lambda E, u=u, half=half, rbk=rbk:
                     E.scalar_tensor_tensor(
                         g[:, u * TQ + half * 512: u * TQ + (half + 1) * 512],
                         g[:, u * TQ + half * 512: u * TQ + (half + 1) * 512],
                         nw[:, u:u + 1], banks[rbk][:, 0:512],
                         ALU.mult, ALU.mult),
                     inc=K.V if (u == NU - 1 and half == 1) else None)
        K.bank_set("b1", K.V)
        K.bank_set("b4", K.V)
        gp_done = K.V.n
        ws.issue(0)
        ws.issue(1)
        for j in range(ws.nslab):
            if j + 2 < ws.nslab:
                ws.issue(j + 2)
            tile = ws.use(j)
            P.wait("tensor", K.V, gp_done)
            for half in range(2):
                bk = 2 + ((2 * j + half) % 2)
                K.bank_wait("tensor", f"b{bk}")
                for k in range(NU):
                    P.op("tensor", lambda E, j=j, k=k, half=half, bk=bk,
                         tile=tile: E.matmul(
                             banks[bk][:, 0:512], tile(k),
                             g[:, k * TQ + half * 512:
                               k * TQ + (half + 1) * 512],
                             start=(k == 0), stop=(k == NU - 1)),
                         inc=K.T if k == NU - 1 else None)
                mm_done = K.T.n
                bi = j % 2
                P.wait("vector", K.T, mm_done)
                P.wait("vector", K.Ls, ev_st[bi])
                P.op("vector", lambda E, bi=bi, half=half, bk=bk: E.tensor_copy(
                    ev[bi][:, half * 512:(half + 1) * 512],
                    banks[bk][:, 0:512]), inc=K.V)
                K.bank_set(f"b{bk}", K.V)
            ws.release(j, K.T)
            P.wait("sync", K.V)
            P.op("sync", lambda E, j=j, t0=t0, bi=j % 2: E.dma_start(
                out=attn0[j * 128:(j + 1) * 128, t0:t0 + TQ],
                in_=ev[bi][:, 0:TQ]), inc=K.Ls, inc_by=16)
            ev_st[j % 2] = K.Ls.n
    K.set_ready("ATTN0")



def ph_ffn0a(K):
    P = K.P
    xn = K.mid("XN", (D, TA), BF16)
    su_ug = K.ext("su_ug", (128, 16), F32)
    wsc_up = K.ext("wsc_up", (128, 64), F32)
    wsc_gate = K.ext("wsc_gate", (128, 64), F32)
    su_dp = K.ext("su_dp", (128, 64), F32)
    h128_d = K.ext("h128", (128, 128), F32)
    w_up = K.ext("w_up_t", (D, INTER), F32)
    w_gate = K.ext("w_gate_t", (D, INTER), F32)
    h1_d = K.mid_out("H1", (INTER, TOK), BF16)
    K.init_common()
    banks = K.banks
    TH = 2048

    xf = K.sbuf("f0_xf", (128, CT * TH), BF16)
    tmpb = K.sbuf("f0_tmp", (128, TH), BF16)
    h128 = K.sbuf("f0_h128", (128, 128), BF16)
    sus = K.sbuf("f0_sus", (128, 16), F32)
    scu = K.sbuf("f0_scu", (128, 64), F32)
    scg = K.sbuf("f0_scg", (128, 64), F32)
    sgt = K.sbuf("f0_sg", (128, 2 * 512), BF16)
    ut = K.sbuf("f0_u", (128, 2 * 512), BF16)
    hv = [K.sbuf(f"f0_hv{i}", (128, TH), BF16) for i in range(2)]
    wsU = WeightStream(K, "f0_wu", w_up, D, INTER, ow=128)
    wsG = WeightStream(K, "f0_wg", w_gate, D, INTER, ow=128)

    P.op("sync", lambda E: E.dma_start(out=sus[:, :], in_=su_ug),
         inc=K.Ls, inc_by=16)
    P.op("sync", lambda E: E.dma_start(out=scu[:, :], in_=wsc_up),
         inc=K.Ls, inc_by=16)
    P.op("sync", lambda E: E.dma_start(out=scg[:, :], in_=wsc_gate),
         inc=K.Ls, inc_by=16)
    sud = K.sbuf("f0_sud", (128, 64), F32)
    P.op("sync", lambda E: E.dma_start(out=sud[:, :], in_=su_dp),
         inc=K.Ls, inc_by=16)
    P.op("gpsimd", lambda E: E.dma_start(out=h128[:, :], in_=h128_d),
         inc=K.Lg, inc_by=16)
    cst_ld = K.Ls.n
    FW1 = 2.0 ** -5.5
    P.wait("vector", K.Ls, cst_ld)
    P.op("vector", lambda E: E.tensor_scalar(
        scg[:, :], scg[:, :], FW1, None, ALU.mult))
    P.op("vector", lambda E: E.tensor_scalar(
        scu[:, :], scu[:, :], FW1, None, ALU.mult), inc=K.V)
    if False:
        dbg2 = K.mid_out("DBG2", (128, 128), F32)
        P.wait("sync", K.V)
        P.op("sync", lambda E: E.dma_start(out=dbg2[:, 0:64],
                                           in_=scu[:, :]), inc=K.Ls, inc_by=16)
        P.op("sync", lambda E: E.dma_start(out=dbg2[:, 64:128],
                                           in_=sud[:, :]), inc=K.Ls, inc_by=16)
    hst = [0, 0]
    dbg3_sb = K.sbuf("f0_dbg3", (128, 1024), F32)

    for th in range(2):
        t0 = HALO + th * TH
        P.wait("sync", K.T)
        K.wait_ready("sync", "XN")
        P.op("sync", lambda E, t0=t0: E.dma_start(
            out=xf[:, :].rearrange("p (u t) -> p u t", u=CT),
            in_=xn[:, t0:t0 + TH].rearrange("(u p) t -> p u t", p=128)),
            inc=K.Ls, inc_by=16)
        P.wait("vector", K.Ls)
        P.wait("vector", K.T)
        for u in range(CT):
            P.op("vector", lambda E, u=u: E.tensor_scalar(
                xf[:, u * TH:(u + 1) * TH], xf[:, u * TH:(u + 1) * TH],
                sus[:, u:u + 1], None, ALU.mult))
        pairs = [(base + off, base + off + d)
                 for d in (1, 2, 4, 8)
                 for base in range(0, CT, 2 * d)
                 for off in range(d)]
        for pi, (a, b) in enumerate(pairs):
            last = (pi == len(pairs) - 1)
            P.op("vector", lambda E, a=a, b=b: E.tensor_tensor(
                tmpb[:, :], xf[:, a * TH:(a + 1) * TH],
                xf[:, b * TH:(b + 1) * TH], ALU.subtract))
            P.op("vector", lambda E, a=a, b=b: E.tensor_tensor(
                xf[:, a * TH:(a + 1) * TH], xf[:, a * TH:(a + 1) * TH],
                xf[:, b * TH:(b + 1) * TH], ALU.add))
            P.op("vector", lambda E, b=b: E.tensor_copy(
                xf[:, b * TH:(b + 1) * TH], tmpb[:, :]),
                inc=K.V if last else None)
        bf_done = K.V.n
        P.wait("tensor", K.V, bf_done)
        P.wait("tensor", K.Lg)
        for u in range(CT):
            for q4 in range(TH // 512):
                bk = (u * 4 + q4) % 2
                K.bank_wait("tensor", f"b{bk}")
                P.op("tensor", lambda E, u=u, q4=q4, bk=bk: E.matmul(
                    banks[bk][:, 0:512], h128[:, :],
                    xf[:, u * TH + q4 * 512: u * TH + (q4 + 1) * 512],
                    start=True, stop=True), inc=K.T)
                P.wait("vector", K.T)
                P.op("vector", lambda E, u=u, q4=q4, bk=bk: E.tensor_copy(
                    xf[:, u * TH + q4 * 512: u * TH + (q4 + 1) * 512],
                    banks[bk][:, 0:512]), inc=K.V)
                K.bank_set(f"b{bk}", K.V)
        x1_done = K.V.n
        wsU.issue(0)
        wsG.issue(0)
        wsU.issue(1)
        wsG.issue(1)
        for j in range(wsU.nslab):
            if j + 2 < wsU.nslab:
                wsU.issue(j + 2)
                wsG.issue(j + 2)
            tU = wsU.use(j)
            tG = wsG.use(j)
            P.wait("tensor", K.V, x1_done)
            bi = j % 2
            for q4 in range(TH // 512):
                bku = 2 + (q4 % 2)
                bkg = 4 + (q4 % 2)
                K.bank_wait("tensor", f"b{bku}")
                K.bank_wait("tensor", f"b{bkg}")
                for k in range(CT):
                    P.op("tensor", lambda E, k=k, q4=q4, bku=bku, tU=tU:
                         E.matmul(banks[bku][:, 0:512], tU(k),
                                  xf[:, k * TH + q4 * 512:
                                     k * TH + (q4 + 1) * 512],
                                  start=(k == 0), stop=(k == CT - 1)))
                for k in range(CT):
                    P.op("tensor", lambda E, k=k, q4=q4, bkg=bkg, tG=tG:
                         E.matmul(banks[bkg][:, 0:512], tG(k),
                                  xf[:, k * TH + q4 * 512:
                                     k * TH + (q4 + 1) * 512],
                                  start=(k == 0), stop=(k == CT - 1)),
                         inc=K.T if k == CT - 1 else None)
                mm_done = K.T.n
                qp = q4 % 2
                P.wait("vector", K.T, mm_done)
                P.wait("vector", K.Ls, hst[bi])
                P.op("vector", lambda E, j=j, bku=bku, qp=qp: E.tensor_scalar(
                    ut[:, qp * 512:(qp + 1) * 512], banks[bku][:, 0:512],
                    scu[:, j:j + 1], sud[:, j:j + 1],
                    ALU.mult, ALU.mult), inc=K.V)
                if False:
                    dbg3 = K.mid_out("DBG3", (128, 1024), F32)
                    P.op("vector", lambda E, bku=bku: E.tensor_copy(
                        dbg3_sb[:, 0:512], banks[bku][:, 0:512]), inc=K.V)
                    P.op("vector", lambda E, qp=qp: E.tensor_copy(
                        dbg3_sb[:, 512:1024],
                        ut[:, qp * 512:(qp + 1) * 512]), inc=K.V)
                    P.wait("sync", K.V)
                    P.op("sync", lambda E: E.dma_start(
                        out=K.dram["DBG3"], in_=dbg3_sb[:, :]),
                        inc=K.Ls, inc_by=16)
                K.bank_set(f"b{bku}", K.V)
                K.bank_wait("scalar", f"sgt{qp}")
                P.wait("scalar", K.T, mm_done)
                P.op("scalar", lambda E, j=j, bkg=bkg, qp=qp: E.activation(
                    sgt[:, qp * 512:(qp + 1) * 512], banks[bkg][:, 0:512],
                    AF.Silu, scale=scg[:, j:j + 1]), inc=K.S)
                K.bank_set(f"b{bkg}", K.S)
                P.wait("vector", K.S)
                P.op("vector", lambda E, q4=q4, bi=bi, qp=qp: E.tensor_tensor(
                    hv[bi][:, q4 * 512:(q4 + 1) * 512],
                    ut[:, qp * 512:(qp + 1) * 512],
                    sgt[:, qp * 512:(qp + 1) * 512], ALU.mult), inc=K.V)
                K.bank_set(f"sgt{qp}", K.V)
            wsU.release(j, K.T)
            wsG.release(j, K.T)
            P.wait("sync", K.V)
            P.op("sync", lambda E, j=j, th=th, bi=bi: E.dma_start(
                out=h1_d[j * 128:(j + 1) * 128, th * TH:(th + 1) * TH],
                in_=hv[bi][:, :]), inc=K.Ls, inc_by=16)
            hst[bi] = K.Ls.n
    K.set_ready("H1")



def ph_ffn0b(K):
    P = K.P
    h1_d = K.mid("H1", (INTER, TOK), BF16)
    wsc_dp = K.ext("wsc_dp", (128, 16), F32)
    h128_d = K.ext("h128", (128, 128), F32)
    w_down = K.ext("w_down_t", (INTER, D), F32)
    ffn0 = K.mid_out("FFN0", (D, TOK), BF16)
    K.init_common()
    banks = K.banks
    TQ = 1024
    NI = INTER // 128

    x2q = K.sbuf("fb_x2", (128, NI * TQ), BF16)
    stg = [K.sbuf(f"fb_st{i}", (128, TQ), BF16) for i in range(2)]
    h128 = K.sbuf("fb_h128", (128, 128), BF16)
    scd = K.sbuf("fb_scd", (128, 16), F32)
    ev = [K.sbuf(f"fb_ev{i}", (128, TQ), BF16) for i in range(2)]
    wsD = WeightStream(K, "fb_wd", w_down, INTER, D, ow=128, nbuf=2)
    P.op("gpsimd", lambda E: E.dma_start(out=h128[:, :], in_=h128_d),
         inc=K.Lg, inc_by=16)
    P.op("sync", lambda E: E.dma_start(out=scd[:, :], in_=wsc_dp),
         inc=K.Ls, inc_by=16)
    cst = K.Ls.n
    P.wait("vector", K.Ls, cst)
    P.op("vector", lambda E: E.tensor_scalar(
        scd[:, :], scd[:, :], 2.0 ** -6.5, None, ALU.mult), inc=K.V)
    ev_st = [0, 0]

    for tq in range(TOK // TQ):
        t0 = tq * TQ
        for u in range(NI):
            bi = u % 2
            P.wait("sync", K.T)
            K.wait_ready("sync", "H1")
            P.op("sync", lambda E, u=u, t0=t0, bi=bi: E.dma_start(
                out=stg[bi][:, :],
                in_=h1_d[u * 128:(u + 1) * 128, t0:t0 + TQ]),
                inc=K.Ls, inc_by=16)
            P.wait("tensor", K.Ls)
            P.wait("tensor", K.Lg)
            for q2 in range(TQ // 512):
                bk = (u * 2 + q2) % 2
                K.bank_wait("tensor", f"b{bk}")
                P.op("tensor", lambda E, u=u, q2=q2, bk=bk, bi=bi: E.matmul(
                    banks[bk][:, 0:512], h128[:, :],
                    stg[bi][:, q2 * 512:(q2 + 1) * 512],
                    start=True, stop=True), inc=K.T)
                P.wait("vector", K.T)
                P.op("vector", lambda E, u=u, q2=q2, bk=bk: E.tensor_copy(
                    x2q[:, u * TQ + q2 * 512: u * TQ + (q2 + 1) * 512],
                    banks[bk][:, 0:512]), inc=K.V)
                K.bank_set(f"b{bk}", K.V)
        x2_done = K.V.n
        wsD.issue(0)
        for j in range(wsD.nslab):
            if j + 1 < wsD.nslab:
                wsD.issue(j + 1)
            tile = wsD.use(j, eng="vector")
            pairs = [(base + off, base + off + d)
                     for d in (1, 2, 4, 8, 16, 32)
                     for base in range(0, NI, 2 * d)
                     for off in range(d)]
            for pi, (a, b) in enumerate(pairs):
                last = (pi == len(pairs) - 1)
                P.op("vector", lambda E, a=a, b=b, tile=tile:
                     E.tensor_tensor(stg[0][:, 0:128], tile(a),
                                     tile(b), ALU.subtract))
                P.op("vector", lambda E, a=a, b=b, tile=tile:
                     E.tensor_tensor(tile(a), tile(a), tile(b),
                                     ALU.add))
                P.op("vector", lambda E, b=b, tile=tile:
                     E.tensor_copy(tile(b), stg[0][:, 0:128]),
                     inc=K.V if last else None)
            wprep = K.V.n
            P.wait("tensor", K.V, wprep)
            bi = j % 2
            for q2 in range(TQ // 512):
                bk = 2 + (q2 % 2)
                K.bank_wait("tensor", f"b{bk}")
                for k in range(NI):
                    P.op("tensor", lambda E, k=k, q2=q2, bk=bk, tile=tile:
                         E.matmul(banks[bk][:, 0:512], tile(k),
                                  x2q[:, k * TQ + q2 * 512:
                                      k * TQ + (q2 + 1) * 512],
                                  start=(k == 0), stop=(k == NI - 1)),
                         inc=K.T if k == NI - 1 else None)
                mm_done = K.T.n
                P.wait("vector", K.T, mm_done)
                P.wait("vector", K.Ls, ev_st[bi])
                P.op("vector", lambda E, j=j, q2=q2, bk=bk, bi=bi:
                     E.tensor_scalar(ev[bi][:, q2 * 512:(q2 + 1) * 512],
                                     banks[bk][:, 0:512],
                                     scd[:, j:j + 1], None, ALU.mult),
                     inc=K.V)
                K.bank_set(f"b{bk}", K.V)
            ws_rel = wsD.release(j, K.T)
            P.wait("sync", K.V)
            P.op("sync", lambda E, j=j, t0=t0, bi=bi: E.dma_start(
                out=ffn0[j * 128:(j + 1) * 128, t0:t0 + TQ],
                in_=ev[bi][:, :]), inc=K.Ls, inc_by=16)
            ev_st[bi] = K.Ls.n
    K.set_ready("FFN0")



def ph_res1(K):
    P = K.P
    x = K.ext("x", (D, TA), F32)
    attn0 = K.mid("ATTN0", (D, TOK), BF16)
    ffn0 = K.mid("FFN0", (D, TOK), BF16)
    x1r = K.mid_out("X1R", (D, TOK), F32)
    K.init_common()
    xt = K.sbuf("r1_xt", (128, CT * 512), F32)
    at = K.sbuf("r1_at", (128, CT * 512), BF16)
    ft = K.sbuf("r1_ft", (128, CT * 512), BF16)
    o1 = K.sbuf("r1_o", (128, CT * 512), F32)
    for tq in range(TOK // 512):
        t0 = tq * 512
        P.wait("sync", K.V)
        K.wait_ready("sync", "ATTN0")
        K.wait_ready("sync", "FFN0")
        P.op("sync", lambda E, t0=t0: E.dma_start(
            out=xt[:, :].rearrange("p (u t) -> p u t", u=CT),
            in_=x[:, HALO + t0:HALO + t0 + 512].rearrange(
                "(u p) t -> p u t", p=128)), inc=K.Ls, inc_by=16)
        P.op("sync", lambda E, t0=t0: E.dma_start(
            out=at[:, :].rearrange("p (u t) -> p u t", u=CT),
            in_=attn0[:, t0:t0 + 512].rearrange("(u p) t -> p u t", p=128)),
            inc=K.Ls, inc_by=16)
        P.op("sync", lambda E, t0=t0: E.dma_start(
            out=ft[:, :].rearrange("p (u t) -> p u t", u=CT),
            in_=ffn0[:, t0:t0 + 512].rearrange("(u p) t -> p u t", p=128)),
            inc=K.Ls, inc_by=16)
        P.wait("vector", K.Ls)
        for u in range(CT):
            P.op("vector", lambda E, u=u: E.tensor_tensor(
                o1[:, u * 512:(u + 1) * 512], at[:, u * 512:(u + 1) * 512],
                ft[:, u * 512:(u + 1) * 512], ALU.add))
        for u in range(CT):
            P.op("vector", lambda E, u=u: E.tensor_tensor(
                o1[:, u * 512:(u + 1) * 512], o1[:, u * 512:(u + 1) * 512],
                xt[:, u * 512:(u + 1) * 512], ALU.add),
                inc=K.V if u == CT - 1 else None)
        P.wait("sync", K.V)
        P.op("sync", lambda E, t0=t0: E.dma_start(
            out=x1r[:, t0:t0 + 512].rearrange("(u p) t -> p u t", p=128),
            in_=o1[:, :].rearrange("p (u t) -> p u t", u=CT)),
            inc=K.Ls, inc_by=16)
    K.set_ready("X1R")



def ph_rms2(K):
    P = K.P
    xn2 = K.mid("XN2", (D, TOK), BF16)
    rms_w = K.ext("rms_w", (128, 16), F32)
    xn2r = K.mid_out("XN2R", (D, TOK), BF16)
    K.init_common()
    banks = K.banks
    xt = K.sbuf("rm_xt", (128, CT * 512), BF16)
    xsq = K.sbuf("rm_xsq", (128, 512), BF16)
    xo = K.sbuf("rm_xo", (128, CT * 512), BF16)
    rw = K.sbuf("rm_rw", (128, 16), F32)
    srow = K.sbuf("rm_srow", (1, 512), F32)
    P.op("sync", lambda E: E.dma_start(out=rw[:, :], in_=rms_w),
         inc=K.Ls, inc_by=16)
    for tq in range(TOK // 512):
        t0 = tq * 512
        P.wait("sync", K.V)
        K.wait_ready("sync", "XN2")
        P.op("sync", lambda E, t0=t0: E.dma_start(
            out=xt[:, :].rearrange("p (u t) -> p u t", u=CT),
            in_=xn2[:, t0:t0 + 512].rearrange("(u p) t -> p u t", p=128)),
            inc=K.Ls, inc_by=16)
        K.bank_wait("tensor", "b0")
        P.wait("vector", K.Ls)
        for u in range(CT):
            P.op("vector", lambda E, u=u: E.tensor_tensor(
                xsq[:, :], xt[:, u * 512:(u + 1) * 512],
                xt[:, u * 512:(u + 1) * 512], ALU.mult), inc=K.V)
            P.wait("tensor", K.V)
            P.op("tensor", lambda E, u=u: E.matmul(
                banks[0][0:1, 0:512], K.ones_col[:, :], xsq[:, :],
                start=(u == 0), stop=(u == CT - 1)),
                inc=K.T if u == CT - 1 else None)
        P.wait("vector", K.T)
        P.op("vector", lambda E: E.tensor_scalar(
            srow[:, :], banks[0][0:1, 0:512], 1.0 / D, 1e-5,
            ALU.mult, ALU.add), inc=K.V)
        K.bank_set("b0", K.V)
        P.wait("scalar", K.V)
        P.op("scalar", lambda E: E.activation(srow[:, :], srow[:, :], AF.Ln))
        P.op("scalar", lambda E: E.activation(srow[:, :], srow[:, :],
                                              AF.Exp, scale=-0.5), inc=K.S)
        K.bank_wait("tensor", "b1")
        P.wait("tensor", K.S)
        P.op("tensor", lambda E: E.matmul(
            banks[1][:, 0:512], K.ones_rowf[:, :], srow[:, :],
            start=True, stop=True), inc=K.T)
        P.wait("vector", K.T)
        for u in range(CT):
            P.op("vector", lambda E, u=u: E.scalar_tensor_tensor(
                xo[:, u * 512:(u + 1) * 512], xt[:, u * 512:(u + 1) * 512],
                rw[:, u:u + 1], banks[1][:, 0:512], ALU.mult, ALU.mult),
                inc=K.V if u == CT - 1 else None)
        K.bank_set("b1", K.V)
        P.wait("sync", K.V)
        P.op("sync", lambda E, t0=t0: E.dma_start(
            out=xn2r[:, t0:t0 + 512].rearrange("(u p) t -> p u t", p=128),
            in_=xo[:, :].rearrange("p (u t) -> p u t", u=CT)),
            inc=K.Ls, inc_by=16)
    K.set_ready("XN2R")



def ph_l1proj(K):
    P = K.P
    xn2r = K.mid("XN2R", (D, TOK), BF16)
    wf_b = K.ext("wf_b", (128, 16), F32)
    for nm in ("wga_t", "wgb_t", "wzg_t", "wz_t", "wf_t"):
        K.ext(nm, (D, D), F32)
    og_d = K.mid_out("OG", (D, TOK), BF16)
    z_d = K.mid_out("Z", (D, TOK), BF16)
    f_d = K.mid_out("F", (D, TOK), BF16)
    K.init_common()
    banks = K.banks
    TH = 2048

    mv = K.sbuf("l1_mv", (128, CT * TH), BF16)
    sga = K.sbuf("l1_sga", (128, CT * TH), BF16)
    fb = K.sbuf("l1_fb", (128, 16), F32)
    ev = [K.sbuf(f"l1_ev{i}", (128, TH), BF16) for i in range(2)]
    ws = WeightStream(K, "l1_w", K.dram["wga_t"], D, D, ow=128)
    P.op("sync", lambda E: E.dma_start(out=fb[:, :], in_=wf_b),
         inc=K.Ls, inc_by=16)
    ev_st = [0, 0]

    for th in range(2):
        t0 = th * TH
        P.wait("sync", K.T)
        P.wait("sync", K.S)
        K.wait_ready("sync", "XN2R")
        P.op("sync", lambda E, t0=t0: E.dma_start(
            out=mv[:, :].rearrange("p (u t) -> p u t", u=CT),
            in_=xn2r[:, t0:t0 + TH].rearrange("(u p) t -> p u t", p=128)),
            inc=K.Ls, inc_by=16)
        mv_ld = K.Ls.n
        for wi, wnm in enumerate(("wga_t", "wgb_t", "wzg_t", "wz_t", "wf_t")):
            ws.dram = K.dram[wnm]
            ws.loaded = [None] * ws.nslab
            ws.issue(0)
            ws.issue(1)
            for j in range(ws.nslab):
                if j + 2 < ws.nslab:
                    ws.issue(j + 2)
                tile = ws.use(j)
                P.wait("tensor", K.Ls, mv_ld)
                bi = j % 2
                for tq in range(TH // 512):
                    bk = tq % 2 if wi % 2 == 0 else 2 + tq % 2
                    K.bank_wait("tensor", f"b{bk}")
                    for k in range(CT):
                        P.op("tensor", lambda E, k=k, tq=tq, bk=bk, tile=tile:
                             E.matmul(banks[bk][:, 0:512], tile(k),
                                      mv[:, k * TH + tq * 512:
                                         k * TH + (tq + 1) * 512],
                                      start=(k == 0), stop=(k == CT - 1)),
                             inc=K.T if k == CT - 1 else None)
                    mm_done = K.T.n
                    if wnm == "wga_t":
                        P.wait("scalar", K.T, mm_done)
                        K.bank_wait("scalar", f"sga{j}_{tq}")
                        P.op("scalar", lambda E, j=j, tq=tq, bk=bk:
                             E.activation(
                                 sga[:, j * TH + tq * 512:
                                     j * TH + (tq + 1) * 512],
                                 banks[bk][:, 0:512], AF.Silu), inc=K.S)
                        K.bank_set(f"b{bk}", K.S)
                    elif wnm == "wzg_t":
                        P.wait("scalar", K.T, mm_done)
                        K.bank_wait("scalar", f"sga{j}_{tq}")
                        P.op("scalar", lambda E, j=j, tq=tq, bk=bk:
                             E.activation(
                                 sga[:, j * TH + tq * 512:
                                     j * TH + (tq + 1) * 512],
                                 banks[bk][:, 0:512], AF.Sigmoid), inc=K.S)
                        K.bank_set(f"b{bk}", K.S)
                    elif wnm == "wf_t":
                        P.wait("scalar", K.T, mm_done)
                        P.wait("scalar", K.Ls, ev_st[bi])
                        P.op("scalar", lambda E, j=j, tq=tq, bk=bk, bi=bi:
                             E.activation(
                                 ev[bi][:, tq * 512:(tq + 1) * 512],
                                 banks[bk][:, 0:512], AF.Sigmoid,
                                 bias=fb[:, j:j + 1]), inc=K.S)
                        K.bank_set(f"b{bk}", K.S)
                    else:
                        P.wait("vector", K.T, mm_done)
                        P.wait("vector", K.S)
                        P.wait("vector", K.Ls, ev_st[bi])
                        P.op("vector", lambda E, j=j, tq=tq, bk=bk, bi=bi:
                             E.tensor_tensor(
                                 ev[bi][:, tq * 512:(tq + 1) * 512],
                                 banks[bk][:, 0:512],
                                 sga[:, j * TH + tq * 512:
                                     j * TH + (tq + 1) * 512], ALU.mult),
                             inc=K.V)
                        K.bank_set(f"b{bk}", K.V)
                        K.bank_set(f"sga{j}_{tq}", K.V)
                ws.release(j, K.T)
                if wnm in ("wgb_t", "wz_t", "wf_t"):
                    dst = {"wgb_t": og_d, "wz_t": z_d, "wf_t": f_d}[wnm]
                    P.wait("sync", K.V)
                    P.wait("sync", K.S)
                    P.op("sync", lambda E, j=j, t0=t0, bi=bi, dst=dst:
                         E.dma_start(
                             out=dst[j * 128:(j + 1) * 128, t0:t0 + TH],
                             in_=ev[bi][:, :]), inc=K.Ls, inc_by=16)
                    ev_st[bi] = K.Ls.n
    K.set_ready("OG")
    K.set_ready("Z")
    K.set_ready("F")



def ph_scan(K):
    P = K.P
    f_d = K.mid("F", (D, TOK), BF16)
    z_d = K.mid("Z", (D, TOK), BF16)
    cc2_in = K.internal("CC2IN", (128, 16), F32)
    cc2_out = K.internal("CC2G", (NCORES * 128, 16), F32)
    K.init_common()
    saved = K._scope
    K._scope = None
    hbuf = K.sbuf("sc_h", (128, 16 * TOK), BF16)
    K._scope = saved
    K.hbuf = hbuf
    fz = [K.sbuf(f"sc_fz{i}", (128, 2 * TOK), BF16) for i in range(2)]
    hl = K.sbuf("sc_hl", (128, 16), F32)
    zer = K.sbuf("sc_zer", (128, 1), F32)
    P.op("vector", lambda E: E.memset(zer[:, :], 0.0), inc=K.V)
    for k in range(16):
        bi = k % 2
        P.wait("sync", K.V)
        K.wait_ready("sync", "F")
        K.wait_ready("sync", "Z")
        P.op("sync", lambda E, k=k, bi=bi: E.dma_start(
            out=fz[bi][:, 0:TOK], in_=f_d[k * 128:(k + 1) * 128, :]),
            inc=K.Ls, inc_by=16)
        P.op("sync", lambda E, k=k, bi=bi: E.dma_start(
            out=fz[bi][:, TOK:2 * TOK], in_=z_d[k * 128:(k + 1) * 128, :]),
            inc=K.Ls, inc_by=16)
        P.wait("vector", K.Ls)
        P.op("vector", lambda E, k=k, bi=bi: E.tensor_tensor_scan(
            hbuf[:, k * TOK:(k + 1) * TOK], fz[bi][:, 0:TOK],
            fz[bi][:, TOK:2 * TOK], zer[:, 0:1], ALU.mult, ALU.add))
        P.op("vector", lambda E, k=k: E.tensor_copy(
            hl[:, k:k + 1], hbuf[:, k * TOK + TOK - 1: k * TOK + TOK]),
            inc=K.V)
    P.wait("sync", K.V)
    P.op("sync", lambda E: E.dma_start(out=cc2_in, in_=hl[:, :]),
         inc=K.Ls, inc_by=16)
    P.wait("gpsimd", K.Ls)
    CC2 = K.P.sem("CC2")
    P.raw("gpsimd", lambda E: E.collective_compute(
        "AllGather", ALU.bypass, replica_groups=[list(range(NCORES))],
        ins=[cc2_in.opt()], outs=[cc2_out.opt()]).then_inc(CC2.h))
    CC2.n += 1
    K.cc2 = CC2



def ph_ffn1a(K):
    P = K.P
    xn2 = K.mid("XN2", (D, TOK), BF16)
    wg1 = K.ext("wg1_t", (D, INTER), I32)
    wu1 = K.ext("wu1_t", (D, INTER), I32)
    sg1 = K.ext("sg1", (128, 64), F32)
    su1 = K.ext("su1", (128, 64), F32)
    hh_d = K.mid_out("HH", (INTER, TOK), BF16)
    K.init_common()
    banks = K.banks
    TH = 2048

    mv = K.sbuf("fn_mv", (128, CT * TH), BF16)
    csb = K.sbuf("fn_csb", (128, TH), F32)
    srow = K.sbuf("fn_srow", (1, TH), F32)
    sgc = K.sbuf("fn_sg", (128, 64), F32)
    suc = K.sbuf("fn_su", (128, 64), F32)
    gt = K.sbuf("fn_gt", (128, 2 * 512), F32)
    sgt = K.sbuf("fn_sgt", (128, 2 * 512), BF16)
    ut2 = K.sbuf("fn_ut", (128, 2 * 512), F32)
    hv = [K.sbuf(f"fn_hv{i}", (128, TH), BF16) for i in range(2)]
    wsG = WeightStream(K, "fn_wg", wg1, D, INTER, ow=128)
    wsU = WeightStream(K, "fn_wu", wu1, D, INTER, ow=128)
    P.op("sync", lambda E: E.dma_start(out=sgc[:, :], in_=sg1),
         inc=K.Ls, inc_by=16)
    P.op("sync", lambda E: E.dma_start(out=suc[:, :], in_=su1),
         inc=K.Ls, inc_by=16)
    hst = [0, 0]

    for th in range(2):
        t0 = th * TH
        P.wait("sync", K.T)
        P.wait("sync", K.V)
        K.wait_ready("sync", "XN2")
        P.op("sync", lambda E, t0=t0: E.dma_start(
            out=mv[:, :].rearrange("p (u t) -> p u t", u=CT),
            in_=xn2[:, t0:t0 + TH].rearrange("(u p) t -> p u t", p=128)),
            inc=K.Ls, inc_by=16)
        mv_ld = K.Ls.n
        for tq in range(TH // 512):
            bk = tq % 2
            K.bank_wait("tensor", f"b{bk}")
            P.wait("tensor", K.Ls, mv_ld)
            for k in range(CT):
                P.op("tensor", lambda E, k=k, tq=tq, bk=bk: E.matmul(
                    banks[bk][0:1, 0:512], K.ones_col[:, :],
                    mv[:, k * TH + tq * 512: k * TH + (tq + 1) * 512],
                    start=(k == 0), stop=(k == CT - 1)),
                    inc=K.T if k == CT - 1 else None)
            P.wait("vector", K.T)
            P.op("vector", lambda E, tq=tq, bk=bk: E.tensor_scalar(
                srow[0:1, tq * 512:(tq + 1) * 512], banks[bk][0:1, 0:512],
                -128.0, None, ALU.mult), inc=K.V)
            K.bank_set(f"b{bk}", K.V)
        P.wait("tensor", K.V)
        for tq in range(TH // 512):
            bk = 2 + tq % 2
            K.bank_wait("tensor", f"b{bk}")
            P.op("tensor", lambda E, tq=tq, bk=bk: E.matmul(
                banks[bk][:, 0:512], K.ones_rowf[:, :],
                srow[0:1, tq * 512:(tq + 1) * 512], start=True, stop=True),
                inc=K.T)
            P.wait("vector", K.T)
            P.op("vector", lambda E, tq=tq, bk=bk: E.tensor_copy(
                csb[:, tq * 512:(tq + 1) * 512], banks[bk][:, 0:512]),
                inc=K.V)
            K.bank_set(f"b{bk}", K.V)
        csb_done = K.V.n
        wsG.issue(0)
        wsU.issue(0)
        wsG.issue(1)
        wsU.issue(1)
        for j in range(wsG.nslab):
            if j + 2 < wsG.nslab:
                wsG.issue(j + 2)
                wsU.issue(j + 2)
            tG = wsG.use(j)
            tU = wsU.use(j)
            P.wait("tensor", K.V, csb_done)
            bi = j % 2
            for tq in range(TH // 512):
                qp = tq % 2
                bkg = qp
                bku = 2 + qp
                K.bank_wait("tensor", f"b{bkg}")
                K.bank_wait("tensor", f"b{bku}")
                for k in range(CT):
                    P.op("tensor", lambda E, k=k, tq=tq, bkg=bkg, tG=tG:
                         E.matmul(banks[bkg][:, 0:512], tG(k),
                                  mv[:, k * TH + tq * 512:
                                     k * TH + (tq + 1) * 512],
                                  start=(k == 0), stop=(k == CT - 1)))
                for k in range(CT):
                    P.op("tensor", lambda E, k=k, tq=tq, bku=bku, tU=tU:
                         E.matmul(banks[bku][:, 0:512], tU(k),
                                  mv[:, k * TH + tq * 512:
                                     k * TH + (tq + 1) * 512],
                                  start=(k == 0), stop=(k == CT - 1)),
                         inc=K.T if k == CT - 1 else None)
                mm_done = K.T.n
                K.bank_wait("vector", f"gt{qp}")
                P.wait("vector", K.T, mm_done)
                P.op("vector", lambda E, tq=tq, qp=qp, bkg=bkg:
                     E.tensor_tensor(gt[:, qp * 512:(qp + 1) * 512],
                                     banks[bkg][:, 0:512],
                                     csb[:, tq * 512:(tq + 1) * 512],
                                     ALU.add), inc=K.V)
                K.bank_set(f"b{bkg}", K.V)
                gt_done = K.V.n
                K.bank_wait("scalar", f"sgt{qp}")
                P.wait("scalar", K.V, gt_done)
                P.op("scalar", lambda E, j=j, qp=qp: E.activation(
                    sgt[:, qp * 512:(qp + 1) * 512],
                    gt[:, qp * 512:(qp + 1) * 512], AF.Silu,
                    scale=sgc[:, j:j + 1]), inc=K.S)
                K.bank_set(f"gt{qp}", K.S)
                P.wait("vector", K.T, mm_done)
                P.op("vector", lambda E, tq=tq, qp=qp, bku=bku:
                     E.tensor_tensor(ut2[:, qp * 512:(qp + 1) * 512],
                                     banks[bku][:, 0:512],
                                     csb[:, tq * 512:(tq + 1) * 512],
                                     ALU.add), inc=K.V)
                K.bank_set(f"b{bku}", K.V)
                P.wait("vector", K.S)
                P.wait("vector", K.Ls, hst[bi])
                P.op("vector", lambda E, tq=tq, qp=qp, j=j, bi=bi:
                     E.scalar_tensor_tensor(
                         hv[bi][:, tq * 512:(tq + 1) * 512],
                         ut2[:, qp * 512:(qp + 1) * 512], suc[:, j:j + 1],
                         sgt[:, qp * 512:(qp + 1) * 512],
                         ALU.mult, ALU.mult), inc=K.V)
                K.bank_set(f"sgt{qp}", K.V)
            wsG.release(j, K.T)
            wsU.release(j, K.T)
            P.wait("sync", K.V)
            P.op("sync", lambda E, j=j, th=th, bi=bi: E.dma_start(
                out=hh_d[j * 128:(j + 1) * 128, th * TH:(th + 1) * TH],
                in_=hv[bi][:, :]), inc=K.Ls, inc_by=16)
            hst[bi] = K.Ls.n
    K.set_ready("HH")



def ph_ffn1b(K):
    P = K.P
    hh_d = K.mid("HH", (INTER, TOK), BF16)
    wd1 = K.ext("wd1_t", (INTER, D), I32)
    sd1 = K.ext("sd1", (128, 16), F32)
    ffn1 = K.mid_out("FFN1", (D, TOK), BF16)
    K.init_common()
    banks = K.banks
    TQ = 1024
    NI = INTER // 128

    hq = K.sbuf("f1_hq", (128, NI * TQ), BF16)
    csb2 = K.sbuf("f1_csb", (128, TQ), F32)
    srow = K.sbuf("f1_srow", (1, TQ), F32)
    sdc = K.sbuf("f1_sd", (128, 16), F32)
    ev = [K.sbuf(f"f1_ev{i}", (128, TQ), BF16) for i in range(2)]
    tv = K.sbuf("f1_tv", (128, 2 * 512), F32)
    wsD = WeightStream(K, "f1_wd", wd1, INTER, D, ow=128, nbuf=2)
    P.op("sync", lambda E: E.dma_start(out=sdc[:, :], in_=sd1),
         inc=K.Ls, inc_by=16)
    ev_st = [0, 0]

    for tq4 in range(TOK // TQ):
        t0 = tq4 * TQ
        K.bank_wait("tensor", "b4")
        K.bank_wait("tensor", "b5")
        for u in range(NI):
            P.wait("sync", K.T)
            K.wait_ready("sync", "HH")
            P.op("sync", lambda E, u=u, t0=t0: E.dma_start(
                out=hq[:, u * TQ:(u + 1) * TQ],
                in_=hh_d[u * 128:(u + 1) * 128, t0:t0 + TQ]),
                inc=K.Ls, inc_by=16)
        P.wait("tensor", K.Ls)
        for u in range(NI):
            P.op("tensor", lambda E, u=u: E.matmul(
                banks[4][0:1, 0:512], K.ones_col[:, :],
                hq[:, u * TQ: u * TQ + 512],
                start=(u == 0), stop=(u == NI - 1)),
                inc=K.T if u == NI - 1 else None)
        P.wait("vector", K.T)
        P.op("vector", lambda E: E.tensor_scalar(
            srow[0:1, 0:512], banks[4][0:1, 0:512], -128.0, None, ALU.mult),
            inc=K.V)
        K.bank_set("b4", K.V)
        P.wait("tensor", K.V)
        for u in range(NI):
            P.op("tensor", lambda E, u=u: E.matmul(
                banks[5][0:1, 0:512], K.ones_col[:, :],
                hq[:, u * TQ + 512: u * TQ + 1024],
                start=(u == 0), stop=(u == NI - 1)),
                inc=K.T if u == NI - 1 else None)
        P.wait("vector", K.T)
        P.op("vector", lambda E: E.tensor_scalar(
            srow[0:1, 512:1024], banks[5][0:1, 0:512], -128.0, None,
            ALU.mult), inc=K.V)
        K.bank_set("b5", K.V)
        P.wait("tensor", K.V)
        for q2 in range(2):
            bk = q2
            K.bank_wait("tensor", f"b{bk}")
            P.op("tensor", lambda E, q2=q2, bk=bk: E.matmul(
                banks[bk][:, 0:512], K.ones_rowf[:, :],
                srow[0:1, q2 * 512:(q2 + 1) * 512], start=True, stop=True),
                inc=K.T)
            P.wait("vector", K.T)
            P.op("vector", lambda E, q2=q2, bk=bk: E.tensor_copy(
                csb2[:, q2 * 512:(q2 + 1) * 512], banks[bk][:, 0:512]),
                inc=K.V)
            K.bank_set(f"b{bk}", K.V)
        csb_done = K.V.n
        if False:
            dbg4 = K.mid_out("DBG4", (128, 2048), F32)
            P.wait("sync", K.V, csb_done)
            P.op("sync", lambda E: E.dma_start(
                out=dbg4[:, 0:1024], in_=csb2[:, :]), inc=K.Ls, inc_by=16)
            P.op("sync", lambda E: E.dma_start(
                out=dbg4[0:1, 1024:2048], in_=srow[:, :]), inc=K.Ls, inc_by=16)
        wsD.issue(0)
        for j in range(wsD.nslab):
            if j + 1 < wsD.nslab:
                wsD.issue(j + 1)
            tile = wsD.use(j)
            P.wait("tensor", K.V, csb_done)
            bi = j % 2
            for q2 in range(2):
                bk = q2
                K.bank_wait("tensor", f"b{bk}")
                for k in range(NI):
                    P.op("tensor", lambda E, k=k, q2=q2, bk=bk, tile=tile:
                         E.matmul(banks[bk][:, 0:512], tile(k),
                                  hq[:, k * TQ + q2 * 512:
                                     k * TQ + (q2 + 1) * 512],
                                  start=(k == 0), stop=(k == NI - 1)),
                         inc=K.T if k == NI - 1 else None)
                mm_done = K.T.n
                P.wait("vector", K.T, mm_done)
                P.wait("vector", K.Ls, ev_st[bi])
                P.op("vector", lambda E, q2=q2, bk=bk: E.tensor_tensor(
                    tv[:, q2 * 512:(q2 + 1) * 512], banks[bk][:, 0:512],
                    csb2[:, q2 * 512:(q2 + 1) * 512], ALU.add), inc=K.V)
                K.bank_set(f"b{bk}", K.V)
                P.op("vector", lambda E, j=j, q2=q2, bi=bi: E.tensor_scalar(
                    ev[bi][:, q2 * 512:(q2 + 1) * 512],
                    tv[:, q2 * 512:(q2 + 1) * 512],
                    sdc[:, j:j + 1], None, ALU.mult), inc=K.V)
            wsD.release(j, K.T)
            P.wait("sync", K.V)
            P.op("sync", lambda E, j=j, t0=t0, bi=bi: E.dma_start(
                out=ffn1[j * 128:(j + 1) * 128, t0:t0 + TQ],
                in_=ev[bi][:, :]), inc=K.Ls, inc_by=16)
            ev_st[bi] = K.Ls.n
    K.set_ready("FFN1")



def ph_fix(K):
    P = K.P
    f_d = K.mid("F", (D, TOK), BF16)
    og_d = K.mid("OG", (D, TOK), BF16)
    selh = K.ext("selH", (128, 8), F32)
    cc2_out = K.internal("CC2G", (NCORES * 128, 16), F32)
    K.init_common()
    hbuf = K.hbuf
    ft = [K.sbuf(f"fx_f{i}", (128, TOK), BF16) for i in range(2)]
    pt = K.sbuf("fx_p", (128, TOK), BF16)
    zbig = K.sbuf("fx_zb", (128, TOK), BF16)
    onec = K.sbuf("fx_onec", (128, 1), F32)
    zer = K.sbuf("fx_zer", (128, 1), F32)
    hb = K.sbuf("fx_hb", (128, 16), F32)
    ga = [K.sbuf(f"fx_ga{i}", (128, 16), F32) for i in range(2)]
    gsl = K.sbuf("fx_gsl", (128, 16), F32)
    sel = K.sbuf("fx_sel", (128, 8), F32)
    P.op("vector", lambda E: E.memset(zer[:, :], 0.0))
    P.op("vector", lambda E: E.memset(onec[:, :], 1.0))
    P.op("vector", lambda E: E.memset(zbig[:, :], 0.0), inc=K.V)
    P.op("sync", lambda E: E.dma_start(out=sel[:, :], in_=selh),
         inc=K.Ls, inc_by=16)
    if hasattr(K, "cc2"):
        P.wait("gpsimd", K.cc2, 1)
    for j in range(NCORES):
        P.wait("gpsimd", K.V)
        P.op("gpsimd", lambda E, j=j: E.dma_start(
            out=gsl[:, :], in_=cc2_out[j * 128:(j + 1) * 128, :]),
            inc=K.Lg, inc_by=16)
        P.wait("vector", K.Lg)
        P.wait("vector", K.Ls)
        if j == 0:
            P.op("vector", lambda E: E.tensor_scalar(
                ga[0][:, :], gsl[:, :], sel[:, 0:1], None, ALU.mult),
                inc=K.V)
        else:
            P.op("vector", lambda E, j=j: E.scalar_tensor_tensor(
                ga[j % 2][:, :], gsl[:, :], sel[:, j:j + 1],
                ga[(j - 1) % 2][:, :], ALU.mult, ALU.add), inc=K.V)
    P.op("vector", lambda E: E.tensor_copy(hb[:, :], ga[7 % 2][:, :]),
         inc=K.V)
    hb_done = K.V.n
    for k in range(16):
        bi = k % 2
        P.wait("sync", K.V)
        K.wait_ready("sync", "F")
        P.op("sync", lambda E, k=k, bi=bi: E.dma_start(
            out=ft[bi][:, :], in_=f_d[k * 128:(k + 1) * 128, :]),
            inc=K.Ls, inc_by=16)
        P.wait("vector", K.Ls)
        P.wait("vector", K.V, hb_done)
        P.op("vector", lambda E, k=k, bi=bi: E.tensor_tensor_scan(
            pt[:, :], ft[bi][:, :], zbig[:, :],
            onec[:, 0:1], ALU.mult, ALU.add))
        P.op("vector", lambda E, k=k: E.scalar_tensor_tensor(
            hbuf[:, k * TOK:(k + 1) * TOK], pt[:, :], hb[:, k:k + 1],
            hbuf[:, k * TOK:(k + 1) * TOK], ALU.mult, ALU.add), inc=K.V)
        P.wait("sync", K.V)
        K.wait_ready("sync", "OG")
        P.op("sync", lambda E, k=k, bi=bi: E.dma_start(
            out=ft[bi][:, :], in_=og_d[k * 128:(k + 1) * 128, :]),
            inc=K.Ls, inc_by=16)
        P.wait("vector", K.Ls)
        P.op("vector", lambda E, k=k, bi=bi: E.tensor_tensor(
            hbuf[:, k * TOK:(k + 1) * TOK], hbuf[:, k * TOK:(k + 1) * TOK],
            ft[bi][:, :], ALU.mult), inc=K.V)
    K.gtd_done = K.V.n


def ph_wout(K):
    P = K.P
    wout = K.ext("wout_t", (D, D), F32)
    wout_b = K.ext("wout_b", (128, 16), F32)
    xn2 = K.mid("XN2", (D, TOK), BF16)
    x1r = K.mid("X1R", (D, TOK), F32)
    ffn1 = K.mid("FFN1", (D, TOK), BF16)
    out = K.out("out", (D, TOK), F32)
    K.init_common()
    banks = K.banks
    hbuf = K.hbuf
    ws = WeightStream(K, "wo_w", wout, D, D, ow=128)
    wb = K.sbuf("wo_wb", (128, 16), F32)
    TH = 2048
    xnt = [K.sbuf(f"wo_xn{i}", (128, TH), BF16) for i in range(2)]
    x1t = [K.sbuf(f"wo_x1{i}", (128, TH), F32) for i in range(2)]
    fnt = [K.sbuf(f"wo_fn{i}", (128, TH), BF16) for i in range(2)]
    o1 = K.sbuf("wo_o1", (128, 2 * 512), F32)
    ev = [K.sbuf(f"wo_ev{i}", (128, TH), F32) for i in range(2)]
    P.op("sync", lambda E: E.dma_start(out=wb[:, :], in_=wout_b),
         inc=K.Ls, inc_by=16)
    ev_st = [0, 0]
    ws.issue(0)
    ws.issue(1)
    for j in range(ws.nslab):
        if j + 2 < ws.nslab:
            ws.issue(j + 2)
        tile = ws.use(j)
        for th in range(2):
            t0 = th * TH
            bi = (2 * j + th) % 2
            P.wait("sync", K.V)
            K.wait_ready("sync", "XN2")
            K.wait_ready("sync", "X1R")
            K.wait_ready("sync", "FFN1")
            P.op("sync", lambda E, j=j, bi=bi, t0=t0: E.dma_start(
                out=xnt[bi][:, :], in_=xn2[j * 128:(j + 1) * 128,
                                           t0:t0 + TH]),
                inc=K.Ls, inc_by=16)
            P.op("sync", lambda E, j=j, bi=bi, t0=t0: E.dma_start(
                out=x1t[bi][:, :], in_=x1r[j * 128:(j + 1) * 128,
                                           t0:t0 + TH]),
                inc=K.Ls, inc_by=16)
            P.op("sync", lambda E, j=j, bi=bi, t0=t0: E.dma_start(
                out=fnt[bi][:, :], in_=ffn1[j * 128:(j + 1) * 128,
                                            t0:t0 + TH]),
                inc=K.Ls, inc_by=16)
            ld_all = K.Ls.n
            P.wait("tensor", K.V, getattr(K, "gtd_done", 0))
            for tq in range(TH // 512):
                gq = th * (TH // 512) + tq
                bk = gq % 2
                qp = gq % 2
                K.bank_wait("tensor", f"b{bk}")
                for k in range(CT):
                    P.op("tensor", lambda E, k=k, gq=gq, bk=bk, tile=tile:
                         E.matmul(banks[bk][:, 0:512], tile(k),
                                  hbuf[:, k * TOK + gq * 512:
                                       k * TOK + (gq + 1) * 512],
                                  start=(k == 0), stop=(k == CT - 1)),
                         inc=K.T if k == CT - 1 else None)
                mm_done = K.T.n
                K.bank_wait("vector", f"o1{qp}")
                P.wait("vector", K.T, mm_done)
                P.op("vector", lambda E, j=j, qp=qp, bk=bk: E.tensor_scalar(
                    o1[:, qp * 512:(qp + 1) * 512], banks[bk][:, 0:512],
                    wb[:, j:j + 1], None, ALU.add), inc=K.V)
                K.bank_set(f"b{bk}", K.V)
                P.wait("vector", K.Ls, ld_all)
                P.wait("vector", K.Ls, ev_st[bi])
                P.op("vector", lambda E, tq=tq, qp=qp, bi=bi: E.tensor_tensor(
                    o1[:, qp * 512:(qp + 1) * 512],
                    o1[:, qp * 512:(qp + 1) * 512],
                    xnt[bi][:, tq * 512:(tq + 1) * 512], ALU.add))
                P.op("vector", lambda E, tq=tq, qp=qp, bi=bi: E.tensor_tensor(
                    o1[:, qp * 512:(qp + 1) * 512],
                    o1[:, qp * 512:(qp + 1) * 512],
                    fnt[bi][:, tq * 512:(tq + 1) * 512], ALU.add))
                P.op("vector", lambda E, tq=tq, qp=qp, bi=bi: E.tensor_tensor(
                    ev[bi][:, tq * 512:(tq + 1) * 512],
                    o1[:, qp * 512:(qp + 1) * 512],
                    x1t[bi][:, tq * 512:(tq + 1) * 512], ALU.add), inc=K.V)
                K.bank_set(f"o1{qp}", K.V)
            P.wait("sync", K.V)
            P.op("sync", lambda E, j=j, bi=bi, t0=t0: E.dma_start(
                out=out[j * 128:(j + 1) * 128, t0:t0 + TH],
                in_=ev[bi][:, :]), inc=K.Ls, inc_by=16)
            ev_st[bi] = K.Ls.n
        ws.release(j, K.T)
    K.set_ready("out")



def build_program(fused_mids=True):
    K = Ctx(fused_mids=fused_mids)
    with K.phase("ln"):
        ph_ln(K, "x", "XN")
    with K.phase("ip"):
        ph_inproj(K)
    with K.phase("cv"):
        ph_conv(K)
    with K.phase("dc"):
        ph_decay(K)
    with K.phase("sd"):
        ph_ssd(K)
    with K.phase("fa"):
        ph_ffn0a(K)
    with K.phase("fb"):
        ph_ffn0b(K)
    with K.phase("s2"):
        ph_ssd2(K)
    with K.phase("go"):
        ph_gateout(K)
    with K.phase("r1"):
        ph_res1(K)
    with K.phase("l2"):
        ph_ln(K, "X1R", "XN2", tcols=TOK, src_is_ext=False)
    with K.phase("rm"):
        ph_rms2(K)
    with K.phase("lp"):
        ph_l1proj(K)
    with K.phase("na"):
        ph_ffn1a(K)
    with K.phase("nb"):
        ph_ffn1b(K)
    with K.phase("sc"):
        ph_scan(K)
    with K.phase("fx"):
        ph_fix(K)
    with K.phase("wo"):
        ph_wout(K)
    K.finish()
    return K


def kernel(**inputs):
    vals = prepare_inputs(inputs)
    K = build_program(fused_mids=True)
    names_in = [n for n, k in K.dram_kind.items() if k == "ExternalInput"]
    in_maps = [{n: vals[n][i] for n in names_in} for i in range(NCORES)]
    res = run_bass_kernel_spmd(K.nc, in_maps, core_ids=list(range(NCORES)))
    out = np.empty((4, 2 * TOK, D), np.float32)
    for i in range(NCORES):
        b, h = i // 2, i % 2
        out[b, h * TOK:(h + 1) * TOK, :] = res.results[i]["out"].T
    K.close()
    return out
